# version 12 (final)
# baseline (speedup 1.0000x reference)
import contextlib
import numpy as np

import concourse.bass as bass
import concourse.mybir as mybir
from concourse.bass_utils import run_bass_kernel_spmd

F32 = mybir.dt.float32
BF16 = mybir.dt.bfloat16
U8 = mybir.dt.uint8
I32 = mybir.dt.int32
U32 = mybir.dt.uint32
AF = mybir.ActivationFunctionType
ALU = mybir.AluOpType
AX = mybir.AxisListType

NCORES = 8
B, T, D = 4, 4096, 1024
TO = 2048
NT = TO // 128
H, DH = 16, 64
DC = D // 128


def _t(a):
    return np.ascontiguousarray(np.asarray(a, dtype=np.float32).T)


def _bias_cols(b):
    return np.ascontiguousarray(np.asarray(b, dtype=np.float32).reshape(-1, 128).T)


def host_consts():
    c = {}
    c["c_ident"] = np.eye(128, dtype=np.float32)
    c["c_identf"] = c["c_ident"]
    bo = np.zeros((128, 2), np.float32)
    bo[:64, 0] = 1.0
    bo[64:, 1] = 1.0
    c["c_blkones"] = bo
    c["c_iota64"] = np.ascontiguousarray(np.broadcast_to(
        np.arange(64, dtype=np.float32)[None, :], (128, 64)))
    inv = 1.0 / 10000.0 ** (np.arange(0, 128, 2, dtype=np.float32) / 128.0)
    fr = np.outer(inv, np.arange(T, dtype=np.float32))
    c["c_ropecos"] = np.cos(fr).astype(np.float32)
    c["c_ropesin"] = np.sin(fr).astype(np.float32)
    return c


CONST_SPECS = {
    "c_ident": ([128, 128], BF16, "gpsimd"),
    "c_identf": ([128, 128], F32, "sync"),
    "c_diag2": ([128, 256], F32, "sync"),
    "c_blkones": ([128, 2], BF16, "gpsimd"),
    "c_iota64": ([128, 64], F32, "sync"),
    "c_ropecos": ([64, T], F32, "sync"),
    "c_ropesin": ([64, T], F32, "sync"),
    "c_ropecos_o": ([64, TO], F32, "sync"),
    "c_ropesin_o": ([64, TO], F32, "sync"),
}
BASE_CONSTS = ["c_ident", "c_identf", "c_diag2", "c_blkones"]


def host_diag2(half):
    tri = np.zeros((128, 128), np.float32)
    tri[np.triu_indices(128, 1)] = -1e30
    full = np.full((128, 128), -1e30, np.float32)
    zero = np.zeros((128, 128), np.float32)
    if half == 0:
        return np.ascontiguousarray(np.concatenate([tri, full], axis=1))
    return np.ascontiguousarray(np.concatenate([zero, tri], axis=1))


def host_rope_own(half):
    inv = 1.0 / 10000.0 ** (np.arange(0, 128, 2, dtype=np.float32) / 128.0)
    pos = own_row_index(half)
    fr = np.outer(inv, pos.astype(np.float32))
    return (np.ascontiguousarray(np.cos(fr).astype(np.float32)),
            np.ascontiguousarray(np.sin(fr).astype(np.float32)))


def own_row_index(core):
    h = core % 2
    idx = []
    for i in range(NT):
        g = 2 * i + h
        idx.extend(range(g * 128, (g + 1) * 128))
    return np.array(idx)


def chunks_of(n, c=128):
    out, o = [], 0
    while o < n:
        out.append((o, min(c, n - o)))
        o += min(c, n - o)
    return out



ENGINES = ("sync", "tensor", "scalar", "vector", "gpsimd")

PIPE = [1]

DBG = [False]
_TAPS = []


def tap(env, name, src_ap, shape, dtype=BF16):
    if not DBG[0]:
        return
    o = env.nc.dram_tensor("dbg_" + name, shape, dtype,
                           kind="ExternalOutput").ap()
    env.begin()
    env.dma("sync", o, src_ap)
    env.wait_all("sync", "dma_s")
    env.flush()


def sbtap(env, name, sb_tile, shape, dtype):
    if not DBG[0]:
        return
    o = env.nc.dram_tensor("dbg_" + name, shape, dtype,
                           kind="ExternalOutput").ap()
    env.begin()
    env.dma("sync", o, sb_tile)
    env.wait_all("sync", "dma_s")
    env.flush()


class Env:
    def __init__(self, nc):
        self.nc = nc
        self.ctx = contextlib.ExitStack()
        self.fence = {}
        self.sem = {}
        for n in ("pe", "act", "vec", "pool", "dma_s", "dma_g", "out", "cc",
                  "bf_vec", "bf_act", "bf_pool"):
            h = self.ctx.enter_context(nc.semaphore("S" + n))
            self.sem[n] = [h, 0]
        self.nphase = 0
        self._q = None
        self.consts = {}
        self.res_stack = None
        for e in ("vector", "scalar", "gpsimd"):
            self.fence[e] = self.ctx.enter_context(
                _sbuf(nc, f"fence_{e}", [1, 16], F32))
        self.echo_sb = {}
        for qn in ("sync", "gpsimd"):
            self.echo_sb[qn] = (
                self.ctx.enter_context(_sbuf(nc, f"echo_a_{qn}", [1, 16], F32)),
                self.ctx.enter_context(_sbuf(nc, f"echo_b_{qn}", [1, 16], F32)))

    def begin(self):
        assert self._q is None
        self._q = {e: [] for e in ENGINES}
        sem = self.sem
        np_ = self.nphase
        if np_ > 0:
            def fw(e, np_=np_):
                e.wait_ge(sem["bf_vec"][0], np_)
                e.wait_ge(sem["bf_act"][0], np_)
                e.wait_ge(sem["bf_pool"][0], np_)
            for e in ENGINES:
                self._q[e].append(fw)

    def flush(self, name="ph"):
        q, self._q = self._q, None
        fence = self.fence
        sem = self.sem
        self.nphase += 1
        fsem = {"vector": "bf_vec", "scalar": "bf_act", "gpsimd": "bf_pool"}
        for e, sn in fsem.items():
            sem[sn][1] += 1
        with self.nc.Block() as blk:
            for e in ENGINES:
                fns = q[e]
                def runner(engine, fns=fns, e=e):
                    for f in fns:
                        f(engine)
                    if e in fsem:
                        if e == "scalar":
                            engine.memzero(fence[e][:, :]).then_inc(
                                sem[fsem[e]][0], 1)
                        else:
                            engine.memset(fence[e][:, :], 0.0).then_inc(
                                sem[fsem[e]][0], 1)
                getattr(blk, e)(runner)

    def op(self, eng, fn, waits=(), inc=None, k=1):
        val = None
        if inc is not None:
            self.sem[inc][1] += k
            val = self.sem[inc][1]
        sem = self.sem
        waits = tuple((sn, v) for (sn, v) in waits if v)
        def run(e, fn=fn, waits=waits, inc=inc, k=k):
            for (sn, v) in waits:
                e.wait_ge(sem[sn][0], v)
            inst = fn(e)
            if inc is not None:
                inst.then_inc(sem[inc][0], k)
        self._q[eng].append(run)
        return val

    def dma(self, queue, out_ap, in_ap, waits=(), echo=False):
        inc = "dma_s" if queue == "sync" else "dma_g"
        dv = self.op(queue, lambda e, o=out_ap, i=in_ap:
                     e.dma_start(out=o, in_=i), waits=waits, inc=inc, k=16)
        if not echo:
            return dv
        ea, eb = self.echo_sb[queue]
        return self.op(queue, lambda e, o=ea, i=eb:
                       e.dma_start(out=o[:, :], in_=i[:, :]),
                       waits=(), inc=inc, k=16)

    def wait_all(self, eng, *sems):
        sem = self.sem
        pairs = tuple((s, sem[s][1]) for s in sems if sem[s][1] > 0)
        def run(e, pairs=pairs):
            for (sn, v) in pairs:
                e.wait_ge(sem[sn][0], v)
        self._q[eng].append(run)

    def cur(self, semname):
        return self.sem[semname][1]


_NAME_N = [0]


def _sbuf(nc, name, shape, dtype):
    _NAME_N[0] += 1
    return nc.sbuf_tensor(f"{name}_{_NAME_N[0]}", shape, dtype)


class _PsumView:

    def __init__(self, handle, cols):
        self._h = handle
        self._cols = cols

    def __getitem__(self, idx):
        return self._h[:, 0:self._cols].__getitem__(idx)


class _PsumGuard:
    def __init__(self, guard, cols):
        self._g = guard
        self._cols = cols

    def __enter__(self):
        return _PsumView(self._g.__enter__(), self._cols)

    def __exit__(self, *a):
        return self._g.__exit__(*a)


def _psum(nc, name, shape, dtype):
    _NAME_N[0] += 1
    esz = 4 if dtype in (F32, I32, U32) else 2
    bank_elems = 2048 // esz
    cols = shape[-1]
    padded = ((cols * esz + 2047) // 2048) * 2048 // esz
    g = nc.psum_tensor(f"{name}_{_NAME_N[0]}", [shape[0], padded], dtype)
    return _PsumGuard(g, cols)


def sb(env, name, shape, dtype):
    st = env.res_stack if env.res_stack is not None else env.ctx
    return st.enter_context(_sbuf(env.nc, name, shape, dtype))



def emit_consts(env, cap, names):
    env.begin()
    for nm in names:
        shape, dt_, qn = CONST_SPECS[nm]
        tile = sb(env, "sb_" + nm, shape, dt_)
        env.consts[nm] = tile
        env.dma(qn, tile[:, :], cap[nm])
    env.wait_all("sync", "dma_s", "dma_g")
    env.wait_all("gpsimd", "dma_s", "dma_g")
    env.flush("consts")


def emit_build_xt(env, xt, x_own, x_pair, full):
    nc = env.nc
    ncols = T if full else TO
    with contextlib.ExitStack() as st:
        xb = [st.enter_context(_sbuf(nc, f"xtb{i}", [128, D], BF16))
              for i in range(3)]
        xbf = [st.enter_context(_sbuf(nc, f"xtf{i}", [128, D], F32))
               for i in range(3)]
        pt = [st.enter_context(_psum(nc, f"xtp{i}", [128, 128], F32))
              for i in range(8)]
        env.begin()
        ident = env.consts["c_ident"]
        if full:
            srcs = [(x_own, i, 2 * i * 128) for i in range(NT)]
            srcs += [(x_pair, i, (2 * i + 1) * 128) for i in range(NT)]
        else:
            srcs = [(x_own, i, i * 128) for i in range(NT)]
        buf_free = [0, 0, 0]
        cast_free = [0, 0, 0]
        dump_free = [0, 0, 0]
        slot_free = [0] * 8
        vv = 0
        xbd = None
        if DBG[0]:
            _NAME_N[0] += 1
            xbd = nc.dram_tensor(f"dbg_xb_{_NAME_N[0]}", [len(srcs) * 128, D],
                                 BF16, kind="ExternalOutput").ap()
        for k, (xd, i, col0) in enumerate(srcs):
            bj = k % 3
            dv = env.dma("gpsimd", xbf[bj][:, :], xd[i * 128:(i + 1) * 128, :],
                         waits=[("act", cast_free[bj])], echo=True)
            cv = env.op("scalar",
                        lambda s, d=xb[bj], sr=xbf[bj]: s.copy(d[:, :],
                                                              sr[:, :]),
                        waits=[("dma_g", dv), ("vec", buf_free[bj]),
                               ("dma_s", dump_free[bj])],
                        inc="act")
            cast_free[bj] = cv
            for c in range(DC):
                sj = (k * DC + c) % 8
                sw = slot_free[sj] + 1 if slot_free[sj] else 0
                pv = env.op("tensor",
                            lambda t, p=pt[sj], b=xb[bj], c=c:
                            t.matmul(p[:, :], b[:, c * 128:(c + 1) * 128],
                                        ident[:, :], start=True, stop=True),
                            waits=[("act", cv), ("vec", sw)],
                            inc="pe")
                dst = xt[:, c * ncols + col0: c * ncols + col0 + 128]
                vv = env.op("vector",
                            lambda v, p=pt[sj], dst=dst: v.tensor_copy(dst, p[:, :]),
                            waits=[("pe", pv)], inc="vec")
                slot_free[sj] = vv
            buf_free[bj] = vv
            if DBG[0]:
                dump_free[bj] = env.dma(
                    "sync", xbd[k * 128:(k + 1) * 128, :], xb[bj][:, :],
                    waits=[("pe", pv)])
        env.wait_all("sync", "vec")
        if DBG[0]:
            _NAME_N[0] += 1
            dmp = nc.dram_tensor(f"dbg_xt_{_NAME_N[0]}", [128, DC * ncols],
                                 BF16, kind="ExternalOutput").ap()
            env.dma("sync", dmp, xt[:, :], waits=[("vec", env.cur("vec"))])
            env.wait_all("sync", "dma_s")
        env.flush("build_xt")


def emit_tok2fm(env, tok_sb, fm_sb, nchunk, ntiles=NT, tok_stride=None):
    nc = env.nc
    ts = tok_stride if tok_stride is not None else nchunk * 128
    with contextlib.ExitStack() as st:
        pt = [st.enter_context(_psum(nc, f"t2fp{i}", [128, 128], F32))
              for i in range(4)]
        env.begin()
        ident = env.consts["c_ident"]
        slot_free = [0] * 4
        for i in range(ntiles):
            for c in range(nchunk):
                sj = (i * nchunk + c) % 4
                pv = env.op("tensor",
                            lambda t, p=pt[sj], i=i, c=c:
                            t.matmul(p[:, :],
                                        tok_sb[:, i * ts + c * 128:
                                               i * ts + (c + 1) * 128],
                                        ident[:, :], start=True, stop=True),
                            waits=[("vec", slot_free[sj]),
                                   ("act", env.cur("act")),
                                   ("vec", env.cur("vec") if i + c == 0 else 0)],
                            inc="pe")
                vv = env.op("vector",
                            lambda v, p=pt[sj], i=i, c=c:
                            v.tensor_copy(
                                fm_sb[:, c * ntiles * 128 + i * 128:
                                      c * ntiles * 128 + (i + 1) * 128],
                                p[:, :]),
                            waits=[("pe", pv)], inc="vec")
                slot_free[sj] = vv
        env.flush("tok2fm")


def emit_proj_fm(env, specs, xt, xt_ncols):
    nc = env.nc
    max_in = max(s["n_in"] for s in specs)
    with contextlib.ExitStack() as st:
        wbuf = [st.enter_context(_sbuf(nc, f"pw{i}", [128, max_in], BF16))
                for i in range(2)]
        stg = [st.enter_context(_sbuf(nc, f"pstg{i}", [128, 4096], BF16))
               for i in range(2)]
        pp = [st.enter_context(_psum(nc, f"ppp{i}", [128, 512], F32))
              for i in range(4)]
        env.begin()
        wfree = [0, 0]
        sfree = [0, 0]
        pfree = [0] * 4
        mg = 0
        for spec in specs:
            n_in, n_out, ntok = spec["n_in"], spec["n_out"], spec["ntok"]
            kcs = chunks_of(n_in)
            ngr = ntok // 512
            for m in range(n_out // 128):
                bj = m % 2
                src = spec["wt"][:, m * 128:(m + 1) * 128]
                nfull, rem = n_in // 128, n_in % 128
                w = [("pe", wfree[bj])]
                dv = 0
                if nfull:
                    dv = env.dma("gpsimd",
                                 wbuf[bj][:, :nfull * 128].rearrange(
                                     "p (kc o) -> p kc o", o=128),
                                 src[:nfull * 128, :].rearrange(
                                     "(kc p) o -> p kc o", p=128), waits=w)
                    w = []
                if rem:
                    dv = env.dma("gpsimd", wbuf[bj][:rem, nfull * 128:n_in],
                                 src[nfull * 128:, :], waits=w)
                sj = m % 2
                for g in range(ngr):
                    pj = mg % 4
                    def mm(t, bw=wbuf[bj], p=pp[pj], g=g, kcs=kcs, ntok=ntok):
                        for j, (off, sz) in enumerate(kcs):
                            r = xt[:sz, j * xt_ncols + g * 512:
                                   j * xt_ncols + g * 512 + 512]
                            inst = t.matmul(p[:, :], bw[:sz, j * 128:j * 128 + 128],
                                            r, start=(j == 0),
                                            stop=(j == len(kcs) - 1))
                        return inst
                    pev = env.op("tensor", mm,
                                 waits=[("dma_g", dv), ("act", pfree[pj])],
                                 inc="pe")
                    bias = spec.get("bias")
                    bap = bias[:, m:m + 1] if bias is not None else 0.0
                    fun = spec.get("act") or AF.Identity
                    av = env.op("scalar",
                                lambda s, p=pp[pj],
                                d=stg[sj][:, g * 512:(g + 1) * 512],
                                bap=bap, fun=fun:
                                s.activation(d, p[:, :], fun, bias=bap),
                                waits=[("pe", pev), ("dma_s", sfree[sj])],
                                inc="act")
                    pfree[pj] = av
                    mg += 1
                wfree[bj] = env.cur("pe")
                sfree[sj] = env.dma(
                    "sync", spec["out"][m * 128:(m + 1) * 128, :],
                    stg[sj][:, :ntok], waits=[("act", env.cur("act"))])
        env.wait_all("sync", "dma_s")
        if DBG[0]:
            for si_, spec in enumerate(specs):
                _NAME_N[0] += 1
                dmp = nc.dram_tensor(f"dbg_proj_{_NAME_N[0]}",
                                     [spec["n_out"], spec["ntok"]], BF16,
                                     kind="ExternalOutput").ap()
                env.dma("sync", dmp, spec["out"][0:spec["n_out"], :])
            env.wait_all("sync", "dma_s")
        env.flush("proj_fm")


def emit_negb(env, qt_dram, kt_dram, negb_tok, scale, ntok_k=T, nheads=H):
    nc = env.nc
    with contextlib.ExitStack() as st:
        tb = [st.enter_context(_sbuf(nc, f"nb{i}", [128, ntok_k], BF16))
              for i in range(2)]
        sq = st.enter_context(_sbuf(nc, "nsq", [128, ntok_k], BF16))
        nq = st.enter_context(_sbuf(nc, "nnq", [2, TO], F32))
        nk = st.enter_context(_sbuf(nc, "nnk", [2, ntok_k], F32))
        mx = st.enter_context(_sbuf(nc, "nmx", [2, 1], F32))
        nb = st.enter_context(_sbuf(nc, "nnb", [2, TO], F32))
        pn = [st.enter_context(_psum(nc, f"npn{i}", [2, 512], F32))
              for i in range(2)]
        ptp = [st.enter_context(_psum(nc, f"nbp{i}", [128, 2], F32))
               for i in range(2)]
        env.begin()
        blkones = env.consts["c_blkones"]
        ident = env.consts["c_identf"]
        tfree = [0, 0]
        pfree = [0, 0]
        prev_pe = 0
        nji = 0
        for hp in range(nheads // 2):
            for (dram, ntok, dstrow) in ((qt_dram, TO, nq), (kt_dram, ntok_k, nk)):
                bj = hp % 2
                dvv = env.dma("sync", tb[bj][:, :ntok],
                              dram[hp * 128:(hp + 1) * 128, :],
                              waits=[("act", tfree[bj])])
                avq = env.op("scalar",
                             lambda s, b=tb[bj], ntok=ntok:
                             s.activation(sq[:, :ntok], b[:, :ntok], AF.Square),
                             waits=[("dma_s", dvv), ("pe", env.cur("pe"))],
                             inc="act")
                tfree[bj] = avq
                for g in range(ntok // 512):
                    nj = nji % 2
                    nji += 1
                    mv = env.op("tensor",
                                lambda t, p=pn[nj], g=g:
                                t.matmul(p[:, :], blkones[:, :],
                                         sq[:, g * 512:(g + 1) * 512],
                                         start=True, stop=True),
                                waits=[("act", avq), ("vec", pfree[nj])],
                                inc="pe")
                    nv = env.op("vector",
                                lambda v, p=pn[nj], d=dstrow, g=g:
                                v.tensor_copy(d[0:2, g * 512:(g + 1) * 512],
                                              p[:, :]),
                                waits=[("pe", mv)], inc="vec")
                    pfree[nj] = nv
            mk = env.op("vector",
                        lambda v: v.tensor_reduce(mx[0:2, :], nk[0:2, :],
                                                  AX.X, ALU.max),
                        waits=[("vec", env.cur("vec"))], inc="vec")
            pv = env.op("vector",
                        lambda v:
                        v.tensor_tensor(nb[0:2, :], nq[0:2, :],
                                        mx[0:2, 0:1].broadcast_to([2, TO]),
                                        ALU.mult),
                        waits=[("vec", mk), ("pe", prev_pe)], inc="vec")
            pv2 = env.op("vector",
                         lambda v: v.tensor_scalar(nb[0:2, :], nb[0:2, :],
                                                   1.0, 1e-30, ALU.mult,
                                                   ALU.max),
                         waits=[("vec", pv)], inc="vec")
            sv0 = env.op("scalar",
                         lambda s: s.activation(nb[0:2, :], nb[0:2, :], AF.Ln),
                         waits=[("vec", pv2)], inc="act")
            sv = env.op("scalar",
                        lambda s: s.activation(nb[0:2, :], nb[0:2, :], AF.Exp,
                                               scale=0.5),
                        waits=[("act", sv0)], inc="act")
            sv2 = env.op("scalar",
                         lambda s: s.mul(nb[0:2, :], nb[0:2, :], -scale),
                         waits=[("act", sv)], inc="act")
            for i in range(NT):
                pj = i % 2
                tv = env.op("tensor",
                            lambda t, p=ptp[pj], i=i:
                            t.matmul(p[:, :], nb[0:2, i * 128:(i + 1) * 128],
                                        ident[0:2, 0:2], start=True, stop=True),
                            waits=[("act", sv2), ("vec", env.cur("vec"))],
                            inc="pe")
                env.op("vector",
                       lambda v, p=ptp[pj], i=i, hp=hp:
                       v.tensor_copy(
                           negb_tok[:, i * nheads + 2 * hp:
                                    i * nheads + 2 * hp + 2], p[:, :]),
                       waits=[("pe", tv)], inc="vec")
            prev_pe = env.cur("pe")
        if DBG[0]:
            _NAME_N[0] += 1
            dmp = nc.dram_tensor(f"dbg_negb_{_NAME_N[0]}", [128, NT * H], F32,
                                 kind="ExternalOutput").ap()
            env.dma("sync", dmp, negb_tok[:, :],
                    waits=[("vec", env.cur("vec"))])
            env.wait_all("sync", "dma_s")
        env.flush("negb")


def emit_attn_std(env, qt_dram, kt_dram, vt_dram, negb_tok, attn_tok, scale,
                  nheads=H):
    nc = env.nc
    with contextlib.ExitStack() as st:
        ktc = [st.enter_context(_sbuf(nc, f"ktc{i}", [128, T], BF16))
               for i in range(2)]
        vtc = [st.enter_context(_sbuf(nc, f"vtc{i}", [128, T], BF16))
               for i in range(2)]
        qtc = [st.enter_context(_sbuf(nc, f"qtc{i}", [128, TO], BF16))
               for i in range(2)]
        vtok = [st.enter_context(_sbuf(nc, f"vtok{i}", [128, T], BF16))
                for i in range(2)]
        a_sb = [st.enter_context(_sbuf(nc, f"a_sb{i}", [128, T], BF16))
                for i in range(2)]
        at_sb = [st.enter_context(_sbuf(nc, f"at_sb{i}", [128, T], BF16))
                 for i in range(2)]
        sums = [st.enter_context(_sbuf(nc, f"sums{i}", [128, 4], F32))
                for i in range(2)]
        rinv = [st.enter_context(_sbuf(nc, f"rinv{i}", [128, 1], F32))
                for i in range(2)]
        psc = [st.enter_context(_psum(nc, f"psc{i}", [128, 1024], F32))
               for i in range(2)]
        pat = [st.enter_context(_psum(nc, f"pat{i}", [128, 512], F32))
               for i in range(2)]
        pav_t = [st.enter_context(_psum(nc, f"pav{i}", [128, 64], F32))
                 for i in range(2)]
        pav = [pav_t[0][:, :], pav_t[1][:, :]]
        env.begin()
        ident = env.consts["c_ident"]
        dmask = env.consts["c_diag2"]
        chunk_free = [0, 0]
        psc_free = [0, 0]
        pat_free = [0, 0]
        pav_free = [0, 0]
        asb_free = [0, 0]
        atsb_free = [0, 0]
        sums_free = [0, 0]
        rinv_free = [0, 0]
        last_avmm = 0
        it = 0
        for c in range(nheads // 2):
            cj = c % 2
            dk = env.dma("sync", ktc[cj][:, :], kt_dram[c * 128:(c + 1) * 128, :],
                         waits=[("act", chunk_free[cj])])
            dvv = env.dma("sync", vtc[cj][:, :], vt_dram[c * 128:(c + 1) * 128, :],
                          waits=[("pe", chunk_free[cj] and last_avmm)])
            dq = env.dma("sync", qtc[cj][:, :], qt_dram[c * 128:(c + 1) * 128, :],
                         waits=[("act", chunk_free[cj])])
            vv = env.cur("vec")
            for g4 in range(8):
                pj = g4 % 2
                tv = 0
                for j in range(4):
                    kb = g4 * 4 + j
                    tv = env.op("tensor",
                                lambda t, kb=kb, j=j, cj=cj, pj=pj:
                                t.matmul(pat[pj][:, j * 128:(j + 1) * 128],
                                            vtc[cj][:, kb * 128:(kb + 1) * 128],
                                            ident[:, :], start=True, stop=True),
                                waits=[("dma_s", dvv), ("vec", pat_free[pj])],
                                inc="pe")
                vv = env.op("vector",
                            lambda v, g4=g4, cj=cj, pj=pj:
                            v.tensor_copy(vtok[cj][:, g4 * 512:(g4 + 1) * 512],
                                          pat[pj][:, :]),
                            waits=[("pe", tv)], inc="vec")
                pat_free[pj] = vv
            vtok_ready = vv
            for hh in range(2):
                head = 2 * c + hh
                p0 = hh * 64
                for i in range(NT):
                    p = it % 2
                    it += 1
                    nkb = 2 * i + 2
                    negb = negb_tok[:, i * nheads + head: i * nheads + head + 1]
                    done = 0
                    si = 0
                    while done < nkb:
                        take = min(8, nkb - done)
                        pj = si % 2
                        def mm(t, cj=cj, p0=p0, i=i, done=done, take=take,
                               pj=pj):
                            inst = None
                            for s0 in range(0, take * 128, 512):
                                w = min(512, take * 128 - s0)
                                inst = t.matmul(
                                    psc[pj][:, s0:s0 + w],
                                    qtc[cj][p0:p0 + 64, i * 128:(i + 1) * 128],
                                    ktc[cj][p0:p0 + 64, done * 128 + s0:
                                            done * 128 + s0 + w],
                                    start=True, stop=True)
                            return inst
                        pev = env.op("tensor", mm,
                                     waits=[("dma_s", dq), ("dma_s", dk),
                                            ("act", psc_free[pj])],
                                     inc="pe")
                        if done + take == nkb:
                            mv = env.op("vector",
                                        lambda v, pj=pj, off=(take - 2) * 128:
                                        v.tensor_tensor(
                                            psc[pj][:, off:off + 256],
                                            psc[pj][:, off:off + 256],
                                            dmask[:, :], ALU.add),
                                        waits=[("pe", pev)], inc="vec")
                            ew = [("vec", mv)]
                        else:
                            ew = [("pe", pev)]
                        av = env.op("scalar",
                                    lambda s, pj=pj, done=done, take=take,
                                    negb=negb, si=si, p=p:
                                    s.activation(
                                        a_sb[p][:, done * 128:(done + take) * 128],
                                        psc[pj][:, :take * 128], AF.Exp,
                                        bias=negb, scale=scale,
                                        accum_out=sums[p][:, si:si + 1]),
                                    waits=ew + [("vec", sums_free[p]),
                                                ("pe", asb_free[p] if PIPE[0] >= 2
                                                 else env.cur("pe")),
                                                ("vec", 0 if PIPE[0] >= 2
                                                 else env.cur("vec"))],
                                    inc="act")
                        psc_free[pj] = av
                        done += take
                        si += 1
                    rv = env.op("vector",
                                lambda v, si=si, p=p:
                                v.tensor_reduce(rinv[p][:, :], sums[p][:, :si],
                                                AX.X, ALU.add),
                                waits=[("act", env.cur("act")),
                                       ("vec", rinv_free[p])], inc="vec")
                    sums_free[p] = rv
                    rv2 = env.op("vector",
                                 lambda v, p=p:
                                 v.reciprocal(rinv[p][:, :], rinv[p][:, :]),
                                 waits=[("vec", rv)], inc="vec")
                    tv = 0
                    for g4 in range((nkb + 3) // 4):
                        jn = min(4, nkb - g4 * 4)
                        pj = g4 % 2
                        for j in range(jn):
                            kb = g4 * 4 + j
                            tv = env.op("tensor",
                                        lambda t, kb=kb, j=j, p=p, pj=pj:
                                        t.matmul(
                                            pat[pj][:, j * 128:(j + 1) * 128],
                                            a_sb[p][:, kb * 128:(kb + 1) * 128],
                                            ident[:, :], start=True, stop=True),
                                        waits=[("act", env.cur("act")),
                                               ("vec", pat_free[pj])],
                                        inc="pe")
                        vv = env.op("vector",
                                    lambda v, g4=g4, jn=jn, p=p, pj=pj:
                                    v.tensor_copy(
                                        at_sb[p][:, g4 * 512:g4 * 512 + jn * 128],
                                        pat[pj][:, :jn * 128]),
                                    waits=[("pe", tv), ("pe", atsb_free[p])],
                                    inc="vec")
                        pat_free[pj] = vv
                    asb_free[p] = env.cur("pe")
                    aj = p
                    def av_mm(t, nkb=nkb, cj=cj, p0=p0, aj=aj, p=p):
                        vt = vtok[cj].rearrange("p (kb d) -> p kb d", d=128)
                        for kb in range(nkb):
                            inst = t.matmul(
                                pav[aj],
                                at_sb[p][:, kb * 128:(kb + 1) * 128],
                                vt[:, kb, p0:p0 + 64],
                                start=(kb == 0), stop=(kb == nkb - 1))
                        return inst
                    pav_v = env.op("tensor", av_mm,
                                   waits=[("vec", env.cur("vec")),
                                          ("vec", pav_free[aj]),
                                          ("act", 0 if PIPE[0] >= 2
                                           else env.cur("act"))],
                                   inc="pe")
                    atsb_free[p] = pav_v
                    last_avmm = pav_v
                    dst = attn_tok[:, i * nheads * 64 + head * 64:
                                   i * nheads * 64 + head * 64 + 64]
                    ev = env.op("vector",
                                lambda v, dst=dst, aj=aj, p=p:
                                v.tensor_scalar(dst, pav[aj], rinv[p][:, 0:1],
                                                None, ALU.mult),
                                waits=[("pe", pav_v), ("vec", rv2)], inc="vec")
                    pav_free[aj] = ev
                    rinv_free[p] = ev
            chunk_free[cj] = env.cur("act")
        if DBG[0]:
            _NAME_N[0] += 1
            dmp = nc.dram_tensor(f"dbg_atok_{_NAME_N[0]}", [128, NT * H * 64],
                                 BF16, kind="ExternalOutput").ap()
            env.dma("sync", dmp, attn_tok[:, :],
                    waits=[("vec", env.cur("vec")), ("act", env.cur("act"))])
            env.wait_all("sync", "dma_s")
        env.flush("attn_core")


def emit_tok_proj_resid_ln(env, act_fm, n_in, wt_dram, x_own, x_out,
                           bias_tile=None, gamma_beta=None, eps=1e-5,
                           add_resid=True):
    nc = env.nc
    kc_n = n_in // 128
    with contextlib.ExitStack() as st:
        wall = st.enter_context(_sbuf(nc, "wall", [128, kc_n * D], BF16))
        xbuf = [st.enter_context(_sbuf(nc, f"xr{i}", [128, D], F32))
                for i in range(2)]
        xn = [st.enter_context(_sbuf(nc, f"xn{i}", [128, D], F32))
              for i in range(2)]
        stats = st.enter_context(_sbuf(nc, "stats", [128, 2 * NT], F32))
        sumsq = st.enter_context(_sbuf(nc, "sumsq", [128, 2 * NT], F32))
        mean = st.enter_context(_sbuf(nc, "mean", [128, NT], F32))
        rstd = st.enter_context(_sbuf(nc, "rstd", [128, NT], F32))
        nmr = st.enter_context(_sbuf(nc, "nmr", [128, NT], F32))
        sq = st.enter_context(_sbuf(nc, "sqs", [128, 512], BF16))
        pp = [st.enter_context(_psum(nc, f"tpp{i}", [128, 512], F32))
              for i in range(4)]
        env.begin()
        dwv = env.dma("gpsimd",
                      wall[:, :].rearrange("p (kc o) -> p kc o", o=D),
                      wt_dram.rearrange("(kc p) o -> p kc o", p=128))
        xfree = [0, 0]
        pfree = [0] * 4
        sqfree = 0
        for i in range(NT):
            bj = i % 2
            dx = 0
            if add_resid:
                dx = env.dma("sync", xbuf[bj][:, :],
                             x_own[i * 128:(i + 1) * 128, :],
                             waits=[("out", xfree[bj])])
            for g in range(2):
                pj = (2 * i + g) % 4
                def mm(t, i=i, g=g, pj=pj):
                    for kc in range(kc_n):
                        inst = t.matmul(
                            pp[pj][:, :],
                            act_fm[:, kc * TO + i * 128: kc * TO + (i + 1) * 128],
                            wall[:, kc * D + g * 512: kc * D + g * 512 + 512],
                            start=(kc == 0), stop=(kc == kc_n - 1))
                    return inst
                pev = env.op("tensor", mm,
                             waits=[("dma_g", dwv), ("vec", pfree[pj])],
                             inc="pe")
                w = [("pe", pev), ("dma_s", dx), ("out", xfree[bj])]
                if bias_tile is not None:
                    bv = env.op("vector",
                                lambda v, pj=pj, g=g:
                                v.tensor_tensor(
                                    pp[pj][:, :], pp[pj][:, :],
                                    bias_tile[:, g * 512:(g + 1) * 512], ALU.add),
                                waits=[("pe", pev)], inc="vec")
                    w.append(("vec", bv))
                if add_resid:
                    vv = env.op("vector",
                                lambda v, pj=pj, bj=bj, g=g, i=i:
                                v.scalar_tensor_tensor(
                                    xn[bj][:, g * 512:(g + 1) * 512],
                                    pp[pj][:, :], 1.0,
                                    xbuf[bj][:, g * 512:(g + 1) * 512],
                                    ALU.mult, ALU.add,
                                    accum_out=stats[:, 2 * i + g: 2 * i + g + 1]),
                                waits=w, inc="vec")
                else:
                    vv = env.op("vector",
                                lambda v, pj=pj, bj=bj, g=g, i=i:
                                v.tensor_scalar(
                                    xn[bj][:, g * 512:(g + 1) * 512],
                                    pp[pj][:, :], 1.0, 0.0, ALU.mult, ALU.add,
                                    accum_out=stats[:, 2 * i + g: 2 * i + g + 1]),
                                waits=w, inc="vec")
                pfree[pj] = vv
                sv = env.op("scalar",
                            lambda s, bj=bj, i=i, g=g:
                            s.activation(sq[:, :], xn[bj][:, g * 512:(g + 1) * 512],
                                         AF.Square,
                                         accum_out=sumsq[:, 2 * i + g:2 * i + g + 1]),
                            waits=[("vec", vv), ("act", sqfree)], inc="act")
                sqfree = sv
            ap1 = _ln_tail(env, i, bj, xn, stats, sumsq, mean, rstd, nmr, eps,
                           dual_stats=True)
            lastsem = "act"
            if gamma_beta is not None:
                gma, bta = gamma_beta
                ap2 = env.op("vector",
                             lambda v, bj=bj:
                             v.tensor_tensor(xn[bj][:, :], xn[bj][:, :],
                                             gma[:, :], ALU.mult),
                             waits=[("act", ap1)], inc="vec")
                ap1 = env.op("vector",
                             lambda v, bj=bj:
                             v.tensor_tensor(xn[bj][:, :], xn[bj][:, :],
                                             bta[:, :], ALU.add),
                             waits=[("vec", ap2)], inc="vec")
                lastsem = "vec"
            xfree[bj] = env.op("sync",
                               lambda e, bj=bj, i=i:
                               e.dma_start(out=x_out[i * 128:(i + 1) * 128, :],
                                           in_=xn[bj][:, :]),
                               waits=[(lastsem, ap1)], inc="out", k=16)
        env.wait_all("sync", "out")
        env.flush("tokproj_ln")



def emit_L1_attn(env, io):
    nc = env.nc
    qt_d = nc.dram_tensor("l1_qt", [D, TO], BF16, kind="Internal").ap()
    kt_d = nc.dram_tensor("l1_kt", [D, T], BF16, kind="Internal").ap()
    vt_d = nc.dram_tensor("l1_vt", [D, T], BF16, kind="Internal").ap()

    bias_cols = sb(env, "l1b", [128, 24], F32)
    env.begin()
    env.dma("sync", bias_cols[:, :], io["L1_attn_bc"])
    env.wait_all("sync", "dma_s")
    env.flush()

    with contextlib.ExitStack() as st:
        xt = st.enter_context(_sbuf(nc, "xtf", [128, DC * T], BF16))
        xto = st.enter_context(_sbuf(nc, "xto", [128, DC * TO], BF16))
        emit_build_xt(env, xt, io["x_ev"], io["x_od"], full=True)
        emit_build_xt(env, xto, io["x_own"], None, full=False)
        wt = io["L1_attn_WT"]
        emit_proj_fm(env, [
            dict(wt=wt[:, 0:D], n_in=D, n_out=D, out=qt_d, ntok=TO,
                 bias=bias_cols[:, 0:8], act=None),
        ], xto, TO)
        emit_proj_fm(env, [
            dict(wt=wt[:, D:2 * D], n_in=D, n_out=D, out=kt_d, ntok=T,
                 bias=bias_cols[:, 8:16], act=None),
            dict(wt=wt[:, 2 * D:3 * D], n_in=D, n_out=D, out=vt_d, ntok=T,
                 bias=bias_cols[:, 16:24], act=None),
        ], xt, T)

    with contextlib.ExitStack() as st2:
        negb_tok = st2.enter_context(_sbuf(nc, "negbt", [128, NT * H], F32))
        attn_tok = st2.enter_context(
            _sbuf(nc, "attn_tok", [128, NT * H * 64], BF16))
        attn_fm = st2.enter_context(
            _sbuf(nc, "attn_fm", [128, DC * TO], BF16))
        emit_negb(env, qt_d, kt_d, negb_tok, 1.0 / 8.0)
        emit_attn_std(env, qt_d, kt_d, vt_d, negb_tok, attn_tok, 1.0 / 8.0)
        emit_tok2fm(env, attn_tok, attn_fm, DC)
        emit_resid_ln_from_fm(env, attn_fm, io["x_own"], io["x_out"])


def emit_resid_ln_from_fm(env, y_fm, x_own, x_out, eps=1e-5):
    nc = env.nc
    with contextlib.ExitStack() as st:
        xbuf = [st.enter_context(_sbuf(nc, f"xr{i}", [128, D], F32))
                for i in range(2)]
        xn = [st.enter_context(_sbuf(nc, f"xn{i}", [128, D], F32))
              for i in range(2)]
        stats = st.enter_context(_sbuf(nc, "stats", [128, 2 * NT], F32))
        sumsq = st.enter_context(_sbuf(nc, "sumsq", [128, 2 * NT], F32))
        mean = st.enter_context(_sbuf(nc, "mean", [128, NT], F32))
        rstd = st.enter_context(_sbuf(nc, "rstd", [128, NT], F32))
        nmr = st.enter_context(_sbuf(nc, "nmr", [128, NT], F32))
        sq = st.enter_context(_sbuf(nc, "sqs", [128, 512], BF16))
        pt = [st.enter_context(_psum(nc, f"rlp{i}", [128, 128], F32))
              for i in range(4)]
        env.begin()
        ident = env.consts["c_ident"]
        xfree = [0, 0]
        ptfree = [0] * 4
        sqfree = 0
        for i in range(NT):
            bj = i % 2
            dx = env.dma("sync", xbuf[bj][:, :], x_own[i * 128:(i + 1) * 128, :],
                         waits=[("out", xfree[bj])])
            for c in range(DC):
                pj = c % 4
                pv = env.op("tensor",
                            lambda t, p=pt[pj], i=i, c=c:
                            t.matmul(p[:, :],
                                        y_fm[:, c * TO + i * 128:
                                             c * TO + (i + 1) * 128],
                                        ident[:, :], start=True, stop=True),
                            waits=[("vec", ptfree[pj])], inc="pe")
                g = c // 4
                vv = env.op("vector",
                            lambda v, p=pt[pj], bj=bj, c=c, i=i, g=g:
                            v.scalar_tensor_tensor(
                                xn[bj][:, c * 128:(c + 1) * 128],
                                p[:, :], 1.0,
                                xbuf[bj][:, c * 128:(c + 1) * 128],
                                ALU.mult, ALU.add),
                            waits=[("pe", pv), ("dma_s", dx),
                                   ("out", xfree[bj])], inc="vec")
                ptfree[pj] = vv
            s1 = env.op("vector",
                        lambda v, bj=bj, i=i:
                        v.tensor_scalar(xn[bj][:, :],
                                        xn[bj][:, :], 1.0, 0.0, ALU.mult,
                                        ALU.add,
                                        accum_out=stats[:, 2 * i:2 * i + 1]),
                        waits=[("vec", env.cur("vec"))], inc="vec")
            sv = env.op("scalar",
                        lambda s, bj=bj, i=i:
                        s.activation(sq[:, :], xn[bj][:, :512], AF.Square,
                                     accum_out=sumsq[:, 2 * i:2 * i + 1]),
                        waits=[("vec", env.cur("vec")), ("act", sqfree)],
                        inc="act")
            sv2 = env.op("scalar",
                         lambda s, bj=bj, i=i:
                         s.activation(sq[:, :], xn[bj][:, 512:], AF.Square,
                                      accum_out=sumsq[:, 2 * i + 1:2 * i + 2]),
                         waits=[("act", sv)], inc="act")
            sqfree = sv2
            ln = _ln_tail(env, i, bj, xn, stats, sumsq, mean, rstd, nmr, eps,
                          dual_stats=False)
            xfree[bj] = env.op("sync",
                               lambda e, bj=bj, i=i:
                               e.dma_start(out=x_out[i * 128:(i + 1) * 128, :],
                                           in_=xn[bj][:, :]),
                               waits=[("act", ln)], inc="out", k=16)
        env.wait_all("sync", "out")
        env.flush("resid_ln")


def _ln_tail(env, i, bj, xn, stats, sumsq, mean, rstd, nmr, eps,
             dual_stats=True):
    if dual_stats:
        c1 = env.op("vector",
                    lambda v, i=i:
                    v.tensor_tensor(mean[:, i:i + 1], stats[:, 2 * i:2 * i + 1],
                                    stats[:, 2 * i + 1:2 * i + 2], ALU.add),
                    waits=[("vec", env.cur("vec")), ("act", env.cur("act"))],
                    inc="vec")
    else:
        c1 = env.op("vector",
                    lambda v, i=i:
                    v.tensor_copy(mean[:, i:i + 1], stats[:, 2 * i:2 * i + 1]),
                    waits=[("vec", env.cur("vec")), ("act", env.cur("act"))],
                    inc="vec")
    c2 = env.op("vector",
                lambda v, i=i:
                v.tensor_scalar(mean[:, i:i + 1], mean[:, i:i + 1],
                                1.0 / D, None, ALU.mult),
                waits=[("vec", c1)], inc="vec")
    c3 = env.op("vector",
                lambda v, i=i:
                v.tensor_tensor(rstd[:, i:i + 1], sumsq[:, 2 * i:2 * i + 1],
                                sumsq[:, 2 * i + 1:2 * i + 2], ALU.add),
                waits=[("vec", c2)], inc="vec")
    c4 = env.op("vector",
                lambda v, i=i:
                v.tensor_scalar(rstd[:, i:i + 1], rstd[:, i:i + 1],
                                1.0 / D, eps, ALU.mult, ALU.add),
                waits=[("vec", c3)], inc="vec")
    c5 = env.op("vector",
                lambda v, i=i:
                v.tensor_tensor(nmr[:, i:i + 1], mean[:, i:i + 1],
                                mean[:, i:i + 1], ALU.mult),
                waits=[("vec", c4)], inc="vec")
    c6 = env.op("vector",
                lambda v, i=i:
                v.tensor_tensor(rstd[:, i:i + 1], rstd[:, i:i + 1],
                                nmr[:, i:i + 1], ALU.subtract),
                waits=[("vec", c5)], inc="vec")
    c7a = env.op("scalar",
                 lambda s, i=i:
                 s.activation(rstd[:, i:i + 1], rstd[:, i:i + 1], AF.Ln),
                 waits=[("vec", c6)], inc="act")
    c7 = env.op("scalar",
                lambda s, i=i:
                s.activation(rstd[:, i:i + 1], rstd[:, i:i + 1], AF.Exp,
                             scale=-0.5),
                waits=[("act", c7a)], inc="act")

    c9 = env.op("vector",
                lambda v, i=i:
                v.tensor_tensor(nmr[:, i:i + 1], mean[:, i:i + 1],
                                rstd[:, i:i + 1], ALU.mult),
                waits=[("act", c7)], inc="vec")
    c10 = env.op("vector",
                 lambda v, i=i:
                 v.tensor_scalar(nmr[:, i:i + 1], nmr[:, i:i + 1],
                                 -1.0, None, ALU.mult),
                 waits=[("vec", c9)], inc="vec")
    return env.op("scalar",
                  lambda s, bj=bj, i=i:
                  s.activation(xn[bj][:, :], xn[bj][:, :], AF.Identity,
                               bias=nmr[:, i:i + 1], scale=rstd[:, i:i + 1]),
                  waits=[("vec", c10)], inc="act")


def emit_L2_attn(env, io):
    nc = env.nc
    qt_d = nc.dram_tensor("l2_qt", [D, TO], BF16, kind="Internal").ap()
    kt_d = nc.dram_tensor("l2_kt", [D, T], BF16, kind="Internal").ap()
    vt_d = nc.dram_tensor("l2_vt", [D, T], BF16, kind="Internal").ap()

    with contextlib.ExitStack() as st:
        xt = st.enter_context(_sbuf(nc, "xtf", [128, DC * T], BF16))
        xto = st.enter_context(_sbuf(nc, "xto", [128, DC * TO], BF16))
        emit_build_xt(env, xt, io["x_ev"], io["x_od"], full=True)
        emit_build_xt(env, xto, io["x_own"], None, full=False)
        emit_proj_fm(env, [
            dict(wt=io["L2_attn_qT"], n_in=D, n_out=D, out=qt_d, ntok=TO),
        ], xto, TO)
        emit_proj_fm(env, [
            dict(wt=io["L2_attn_kT"], n_in=D, n_out=D, out=kt_d, ntok=T),
            dict(wt=io["L2_attn_vT"], n_in=D, n_out=D, out=vt_d, ntok=T),
        ], xt, T)

    with contextlib.ExitStack() as st2:
        negb_tok = st2.enter_context(_sbuf(nc, "negbt", [128, NT * H], F32))
        attn_tok = st2.enter_context(
            _sbuf(nc, "attn_tok", [128, NT * H * 64], BF16))
        attn_fm = st2.enter_context(
            _sbuf(nc, "attn_fm", [128, DC * TO], BF16))
        tap(env, "qt", qt_d, [D, TO])
        tap(env, "kt", kt_d, [D, T])
        emit_negb(env, qt_d, kt_d, negb_tok, 1.0 / 8.0)
        sbtap(env, "negb", negb_tok[:, :], [128, NT * H], F32)
        emit_attn_std(env, qt_d, kt_d, vt_d, negb_tok, attn_tok, 1.0 / 8.0)
        sbtap(env, "attntok", attn_tok[:, :], [128, NT * H * 64], BF16)
        emit_tok2fm(env, attn_tok, attn_fm, DC)
        emit_tok_proj_resid_ln(env, attn_fm, D, io["L2_attn_oT"],
                               io["x_own"], io["x_out"])


def emit_rope_gelu(env, src_d, dst_d, ntok, cos_t, sin_t):
    nc = env.nc
    with contextlib.ExitStack() as st:
        lo = st.enter_context(_sbuf(nc, "rp_lo", [64, ntok], F32))
        hi = st.enter_context(_sbuf(nc, "rp_hi", [64, ntok], F32))
        ta = st.enter_context(_sbuf(nc, "rp_ta", [64, ntok], F32))
        tb = st.enter_context(_sbuf(nc, "rp_tb", [64, ntok], F32))
        oa = st.enter_context(_sbuf(nc, "rp_oa", [64, ntok], BF16))
        ob = st.enter_context(_sbuf(nc, "rp_ob", [64, ntok], BF16))
        env.begin()
        d1 = env.dma("gpsimd", lo[:, :], src_d[0:64, :])
        d2 = env.dma("gpsimd", hi[:, :], src_d[64:128, :])
        m1 = env.op("vector", lambda v: v.tensor_tensor(
            ta[:, :], lo[:, :], cos_t[:, :], ALU.mult),
            waits=[("dma_g", d1)], inc="vec")
        m2 = env.op("vector", lambda v: v.tensor_tensor(
            tb[:, :], hi[:, :], sin_t[:, :], ALU.mult),
            waits=[("dma_g", d2)], inc="vec")
        m3 = env.op("vector", lambda v: v.tensor_tensor(
            ta[:, :], ta[:, :], tb[:, :], ALU.subtract),
            waits=[("vec", m2)], inc="vec")
        g1 = env.op("scalar", lambda s: s.activation(oa[:, :], ta[:, :], AF.Gelu),
                    waits=[("vec", m3)], inc="act")
        m4 = env.op("vector", lambda v: v.tensor_tensor(
            tb[:, :], hi[:, :], cos_t[:, :], ALU.mult),
            waits=[("vec", m3)], inc="vec")
        m5 = env.op("vector", lambda v: v.tensor_tensor(
            ta[:, :], lo[:, :], sin_t[:, :], ALU.mult),
            waits=[("vec", m4), ("act", g1)], inc="vec")
        m6 = env.op("vector", lambda v: v.tensor_tensor(
            tb[:, :], tb[:, :], ta[:, :], ALU.add),
            waits=[("vec", m5)], inc="vec")
        g2 = env.op("scalar", lambda s: s.activation(ob[:, :], tb[:, :], AF.Gelu),
                    waits=[("vec", m6)], inc="act")
        env.dma("sync", dst_d[0:64, :], oa[:, :], waits=[("act", g1)])
        env.dma("sync", dst_d[64:128, :], ob[:, :], waits=[("act", g2)])
        env.wait_all("sync", "dma_s")
        env.flush("rope")


def emit_L3_attn(env, io):
    nc = env.nc
    cq_d = nc.dram_tensor("l3_cq", [256, TO], BF16, kind="Internal").ap()
    ckv_d = nc.dram_tensor("l3_ckv", [256, T], BF16, kind="Internal").ap()
    qr_d = nc.dram_tensor("l3_qr", [128, TO], BF16, kind="Internal").ap()
    kr_d = nc.dram_tensor("l3_kr", [128, T], BF16, kind="Internal").ap()
    qt_d = nc.dram_tensor("l3_qt", [D, TO], BF16, kind="Internal").ap()
    kt_d = nc.dram_tensor("l3_kt", [D, T], BF16, kind="Internal").ap()
    vt_d = nc.dram_tensor("l3_vt", [D, T], BF16, kind="Internal").ap()

    bias = sb(env, "l3bias", [128, 28], F32)
    bo_bc = sb(env, "l3bo", [128, D], F32)
    env.begin()
    env.dma("sync", bias[:, :], io["L3_attn_bias"])
    env.dma("sync", bo_bc[:, :], io["L3_attn_bo_bc"])
    env.wait_all("sync", "dma_s")
    env.flush()

    with contextlib.ExitStack() as st:
        xt = st.enter_context(_sbuf(nc, "xtf", [128, DC * T], BF16))
        xto = st.enter_context(_sbuf(nc, "xto", [128, DC * TO], BF16))
        emit_build_xt(env, xt, io["x_ev"], io["x_od"], full=True)
        emit_build_xt(env, xto, io["x_own"], None, full=False)
        emit_proj_fm(env, [
            dict(wt=io["L3_attn_wq"], n_in=D, n_out=256, out=cq_d, ntok=TO,
                 bias=bias[:, 0:2], act=AF.Gelu),
            dict(wt=io["L3_attn_wqr"], n_in=D, n_out=128, out=qr_d, ntok=TO,
                 bias=bias[:, 9:10]),
        ], xto, TO)
        emit_proj_fm(env, [
            dict(wt=io["L3_attn_wkv"], n_in=D, n_out=256, out=ckv_d, ntok=T,
                 bias=bias[:, 10:12], act=AF.Gelu),
            dict(wt=io["L3_attn_wkr"], n_in=D, n_out=128, out=kr_d, ntok=T,
                 bias=bias[:, 12:13]),
        ], xt, T)

    with contextlib.ExitStack() as st:
        cqs = st.enter_context(_sbuf(nc, "cqs", [128, 2 * TO], BF16))
        ckvs = st.enter_context(_sbuf(nc, "ckvs", [128, 2 * T], BF16))
        env.begin()
        env.dma("sync", cqs[:, :].rearrange("p (c t) -> p c t", c=2),
                cq_d.rearrange("(c p) t -> p c t", p=128))
        env.dma("sync", ckvs[:, :].rearrange("p (c t) -> p c t", c=2),
                ckv_d.rearrange("(c p) t -> p c t", p=128))
        env.wait_all("sync", "dma_s")
        env.wait_all("tensor", "dma_s")
        env.flush()
        emit_proj_fm(env, [
            dict(wt=io["L3_attn_wqc"], n_in=256, n_out=896, out=qt_d[0:896, :],
                 ntok=TO, bias=bias[:, 2:9], act=AF.Gelu),
        ], cqs, TO)
        emit_proj_fm(env, [
            dict(wt=io["L3_attn_wkc"], n_in=256, n_out=896, out=kt_d[0:896, :],
                 ntok=T, bias=bias[:, 13:20], act=AF.Gelu),
            dict(wt=io["L3_attn_wv"], n_in=256, n_out=D, out=vt_d, ntok=T,
                 bias=bias[:, 20:28]),
        ], ckvs, T)

    emit_rope_gelu(env, qr_d, qt_d[896:1024, :], TO,
                   env.consts["c_ropecos_o"], env.consts["c_ropesin_o"])
    emit_rope_gelu(env, kr_d, kt_d[896:1024, :], T,
                   env.consts["c_ropecos"], env.consts["c_ropesin"])


    with contextlib.ExitStack() as st2:
        negb_tok = st2.enter_context(_sbuf(nc, "negbt", [128, NT * H], F32))
        attn_tok = st2.enter_context(
            _sbuf(nc, "attn_tok", [128, NT * H * 64], BF16))
        attn_fm = st2.enter_context(
            _sbuf(nc, "attn_fm", [128, DC * TO], BF16))
        emit_negb(env, qt_d, kt_d, negb_tok, 1.0 / 8.0)
        emit_attn_std(env, qt_d, kt_d, vt_d, negb_tok, attn_tok, 1.0 / 8.0)
        emit_tok2fm(env, attn_tok, attn_fm, DC)
        emit_tok_proj_resid_ln(env, attn_fm, D, io["L3_attn_wo"],
                               io["x_own"], io["x_out"], bias_tile=bo_bc)


def emit_l0_stats(env, qt_d, kt_d, mu_tok, s2_tok):
    nc = env.nc
    with contextlib.ExitStack() as st:
        ktc = st.enter_context(_sbuf(nc, "st_kt", [128, T], BF16))
        qtc = st.enter_context(_sbuf(nc, "st_qt", [128, TO], BF16))
        ktok = st.enter_context(_sbuf(nc, "st_ktok", [128, T], BF16))
        qtok = st.enter_context(_sbuf(nc, "st_qtok", [128, TO], BF16))
        qh = st.enter_context(_sbuf(nc, "st_qh", [64, TO], BF16))
        ksb = st.enter_context(_sbuf(nc, "st_ks", [128, 1], BF16))
        ksh = st.enter_context(_sbuf(nc, "st_ksh", [64, 1], BF16))
        msb = st.enter_context(_sbuf(nc, "st_m", [64, 64], BF16))
        tmp = st.enter_context(_sbuf(nc, "st_tmp", [128, 64], F32))
        onec = st.enter_context(_sbuf(nc, "st_one", [128, 1], BF16))
        pt = st.enter_context(_psum(nc, "st_pt", [128, 512], F32))
        pm = st.enter_context(_psum(nc, "st_pm", [64, 64], F32))
        pks = st.enter_context(_psum(nc, "st_pks", [128, 1], F32))
        pmu = st.enter_context(_psum(nc, "st_pmu", [128, NT], F32))
        pt2 = st.enter_context(_psum(nc, "st_pt2", [128, 64], F32))
        env.begin()
        ident = env.consts["c_ident"]
        mset = env.op("vector", lambda v: v.memset(onec[:, :], 1.0), inc="vec")
        for c in range(DC):
            prev_pe = env.cur("pe")
            dk = env.dma("sync", ktc[:, :], kt_d[c * 128:(c + 1) * 128, :],
                         waits=[("pe", prev_pe)])
            dq = env.dma("sync", qtc[:, :], qt_d[c * 128:(c + 1) * 128, :],
                         waits=[("pe", prev_pe)])
            for (src, dst, n, dwait) in ((ktc, ktok, 32, dk), (qtc, qtok, 16, dq)):
                for g4 in range(n // 4):
                    tv = 0
                    for j in range(4):
                        kb = g4 * 4 + j
                        tv = env.op("tensor",
                                    lambda t, src=src, kb=kb, j=j:
                                    t.matmul(pt[:, j * 128:(j + 1) * 128],
                                                src[:, kb * 128:(kb + 1) * 128],
                                                ident[:, :], start=True, stop=True),
                                    waits=[("dma_s", dwait),
                                           ("vec", env.cur("vec"))], inc="pe")
                    env.op("vector",
                           lambda v, dst=dst, g4=g4:
                           v.tensor_copy(dst[:, g4 * 512:(g4 + 1) * 512],
                                         pt[:, :]),
                           waits=[("pe", tv)], inc="vec")
            ktok_v = env.cur("vec")
            def ks_mm(t):
                for kb in range(32):
                    inst = t.matmul(pks[:, :], ktok[:, kb * 128:(kb + 1) * 128],
                                    onec[:, :], start=(kb == 0), stop=(kb == 31))
                return inst
            kv = env.op("tensor", ks_mm, waits=[("vec", ktok_v), ("vec", mset),
                                                ("act", env.cur("act"))],
                        inc="pe")
            kse = env.op("vector", lambda v: v.tensor_copy(ksb[:, :], pks[:, :]),
                         waits=[("pe", kv)], inc="vec")
            for hh in range(2):
                head = 2 * c + hh
                o0 = hh * 64
                def m_mm(t, o0=o0):
                    for kb in range(32):
                        sl = ktok[:, kb * 128 + o0: kb * 128 + o0 + 64]
                        inst = t.matmul(pm[:, :], sl, sl, start=(kb == 0),
                                        stop=(kb == 31))
                    return inst
                mv = env.op("tensor", m_mm,
                            waits=[("vec", env.cur("vec")),
                                   ("act", env.cur("act"))], inc="pe")
                me = env.op("vector", lambda v: v.tensor_copy(msb[:, :], pm[:, :]),
                            waits=[("pe", mv)], inc="vec")
                qc = env.op("vector",
                            lambda v, o0=o0:
                            v.tensor_copy(qh[:, :], qtc[o0:o0 + 64, :]),
                            waits=[("dma_s", dq), ("pe", env.cur("pe"))],
                            inc="vec")
                kc2 = env.op("vector",
                             lambda v, o0=o0:
                             v.tensor_copy(ksh[:, :], ksb[o0:o0 + 64, :]),
                             waits=[("vec", kse), ("pe", env.cur("pe"))],
                             inc="vec")
                def mu_mm(t):
                    for i in range(NT):
                        inst = t.matmul(pmu[:, i:i + 1],
                                        qh[:, i * 128:(i + 1) * 128],
                                        ksh[:, :], start=True, stop=True)
                    return inst
                muv = env.op("tensor", mu_mm,
                             waits=[("vec", qc), ("vec", kc2),
                                    ("act", env.cur("act"))], inc="pe")
                ev1 = env.op("scalar",
                             lambda s, head=head:
                             s.copy(mu_tok[:, head:NT * 16:16], pmu[:, :]),
                             waits=[("pe", muv)], inc="act")
                for i in range(NT):
                    tv2 = env.op("tensor",
                                 lambda t, i=i:
                                 t.matmul(pt2[:, :], qh[:, i * 128:(i + 1) * 128],
                                          msb[:, :], start=True, stop=True),
                                 waits=[("vec", me), ("vec", env.cur("vec")),
                                        ("act", ev1)], inc="pe")
                    vv2 = env.op("vector",
                                 lambda v, i=i, o0=o0:
                                 v.tensor_tensor(
                                     tmp[:, :], pt2[:, :],
                                     qtok[:, i * 128 + o0: i * 128 + o0 + 64],
                                     ALU.mult),
                                 waits=[("pe", tv2)], inc="vec")
                    env.op("vector",
                           lambda v, i=i, head=head:
                           v.tensor_reduce(
                               s2_tok[:, i * 16 + head: i * 16 + head + 1],
                               tmp[:, :64], AX.X, ALU.add),
                           waits=[("vec", vv2)], inc="vec")
        env.flush("l0stats")


def emit_l0_zparams(env, mu_tok, s2_tok, zs_tok, zb_tok, scale=0.125,
                    eps=1e-5, dump=None):
    nc = env.nc
    NCOLS = NT * 16
    with contextlib.ExitStack() as st:
        t1 = st.enter_context(_sbuf(nc, "zp1", [128, NCOLS], F32))
        env.begin()
        a = env.op("vector", lambda v: v.tensor_tensor(
            t1[:, :], mu_tok[:, :], mu_tok[:, :], ALU.mult), inc="vec")
        b = env.op("vector", lambda v: v.tensor_scalar(
            t1[:, :], t1[:, :], 1.0 / T, 0.0, ALU.mult, ALU.add),
            waits=[("vec", a)], inc="vec")
        c = env.op("vector", lambda v: v.tensor_tensor(
            t1[:, :], s2_tok[:, :], t1[:, :], ALU.subtract),
            waits=[("vec", b)], inc="vec")
        d = env.op("vector", lambda v: v.tensor_scalar(
            t1[:, :], t1[:, :], 1.0 / (T - 1), 0.0, ALU.mult, ALU.add),
            waits=[("vec", c)], inc="vec")
        d2 = env.op("vector", lambda v: v.tensor_scalar(
            t1[:, :], t1[:, :], 1.0, 1e-30, ALU.mult, ALU.max),
            waits=[("vec", d)], inc="vec")
        e0 = env.op("scalar", lambda s: s.activation(t1[:, :], t1[:, :], AF.Ln),
                    waits=[("vec", d2)], inc="act")
        e = env.op("scalar", lambda s: s.activation(t1[:, :], t1[:, :], AF.Exp,
                                                    scale=0.5),
                   waits=[("act", e0)], inc="act")
        f = env.op("vector", lambda v: v.tensor_scalar(
            t1[:, :], t1[:, :], scale, eps, ALU.mult, ALU.add),
            waits=[("act", e)], inc="vec")
        g = env.op("vector", lambda v: v.reciprocal(zs_tok[:, :], t1[:, :]),
                   waits=[("vec", f)], inc="vec")
        h = env.op("vector", lambda v: v.tensor_scalar(
            zs_tok[:, :], zs_tok[:, :], scale, 0.0, ALU.mult, ALU.add),
            waits=[("vec", g)], inc="vec")
        i1 = env.op("vector", lambda v: v.tensor_tensor(
            zb_tok[:, :], zs_tok[:, :], mu_tok[:, :], ALU.mult),
            waits=[("vec", h)], inc="vec")
        j1 = env.op("vector", lambda v: v.tensor_scalar(
            zb_tok[:, :], zb_tok[:, :], -1.0 / T, 0.0, ALU.mult, ALU.add),
            waits=[("vec", i1)], inc="vec")
        if dump is not None:
            env.dma("sync", dump[0], zs_tok[:, :], waits=[("vec", j1)])
            env.dma("sync", dump[1], zb_tok[:, :], waits=[("vec", j1)])
            env.wait_all("sync", "dma_s")
        env.flush("zparams")


def emit_attn_quartet(env, qt_d, q2t_d, kt_d, k2t_d, vt_d, zs1, zb1, zs2, zb2,
                      mixb, qsb, attn_tok, wcst):
    nc = env.nc
    with contextlib.ExitStack() as st:
        ktc = [st.enter_context(_sbuf(nc, "q_kt0", [128, T], BF16))]
        k2c = [st.enter_context(_sbuf(nc, "q_k20", [128, T], BF16))]
        vtc = [st.enter_context(_sbuf(nc, "q_vt0", [128, T], BF16))]
        qtc = [st.enter_context(_sbuf(nc, "q_qt0", [128, TO], BF16))]
        q2c = [st.enter_context(_sbuf(nc, "q_q20", [128, TO], BF16))]
        vtok = [st.enter_context(_sbuf(nc, "q_vk0", [128, T], BF16))]
        z1sb = [st.enter_context(_sbuf(nc, f"q_z1{i}", [128, 1024], BF16))
                for i in range(2)]
        wsb = [st.enter_context(_sbuf(nc, f"q_w{i}", [128, 1024], BF16))
               for i in range(2)]
        s_sb = [st.enter_context(_sbuf(nc, f"q_s{i}", [128, T], BF16))
                for i in range(2)]
        a_sb = [st.enter_context(_sbuf(nc, f"q_a{i}", [128, T], BF16))
                for i in range(2)]
        at_sb = [st.enter_context(_sbuf(nc, f"q_at{i}", [128, T], BF16))
                 for i in range(2)]
        maxc = [st.enter_context(_sbuf(nc, f"q_mx{i}", [128, 4], F32))
                for i in range(2)]
        negmax = [st.enter_context(_sbuf(nc, f"q_nm{i}", [128, 1], F32))
                  for i in range(2)]
        sums = [st.enter_context(_sbuf(nc, f"q_sm{i}", [128, 4], F32))
                for i in range(2)]
        rinv = [st.enter_context(_sbuf(nc, f"q_ri{i}", [128, 1], F32))
                for i in range(2)]
        psc1 = st.enter_context(_psum(nc, "q_p1", [128, 1024], F32))
        psc2 = st.enter_context(_psum(nc, "q_p2", [128, 1024], F32))
        pat = [st.enter_context(_psum(nc, f"q_pat{i}", [128, 512], F32))
               for i in range(2)]
        pav_t = [st.enter_context(_psum(nc, f"q_pav{i}", [128, 64], F32))
                 for i in range(2)]
        pav = [pav_t[0][:, :], pav_t[1][:, :]]
        env.begin()
        ident = env.consts["c_ident"]
        dmask = env.consts["c_diag2"]
        chunk_free = [0, 0]
        p1_free = 0
        p2_free = 0
        z1_free = [0, 0]
        w_free = [0, 0]
        pat_free = [0, 0]
        pav_free = [0, 0]
        ssb_free = [0, 0]
        asb_free = [0, 0]
        atsb_free = [0, 0]
        maxc_free = [0, 0]
        nm_free = [0, 0]
        sums_free = [0, 0]
        rinv_free = [0, 0]
        last_avmm = 0
        it = 0
        gj_i = 0
        for c in range(8):
            cj = 0
            cw = [("act", chunk_free[cj]), ("pe", c and env.cur("pe")),
                  ("vec", c and env.cur("vec"))]
            dk = env.dma("sync", ktc[cj][:, :], kt_d[c * 128:(c + 1) * 128, :],
                         waits=cw)
            dk2 = env.dma("sync", k2c[cj][:, :], k2t_d[c * 128:(c + 1) * 128, :],
                          waits=cw)
            dvv = env.dma("sync", vtc[cj][:, :], vt_d[c * 128:(c + 1) * 128, :],
                          waits=cw)
            dq = env.dma("sync", qtc[cj][:, :], qt_d[c * 128:(c + 1) * 128, :],
                         waits=cw)
            dq2 = env.dma("sync", q2c[cj][:, :], q2t_d[c * 128:(c + 1) * 128, :],
                          waits=cw)
            for g4 in range(8):
                pj = g4 % 2
                tv = 0
                for j in range(4):
                    kb = g4 * 4 + j
                    tv = env.op("tensor",
                                lambda t, kb=kb, j=j, cj=cj, pj=pj:
                                t.matmul(pat[pj][:, j * 128:(j + 1) * 128],
                                            vtc[cj][:, kb * 128:(kb + 1) * 128],
                                            ident[:, :], start=True, stop=True),
                                waits=[("dma_s", dvv), ("vec", pat_free[pj])],
                                inc="pe")
                vv = env.op("vector",
                            lambda v, g4=g4, cj=cj, pj=pj:
                            v.tensor_copy(vtok[cj][:, g4 * 512:(g4 + 1) * 512],
                                          pat[pj][:, :]),
                            waits=[("pe", tv)], inc="vec")
                pat_free[pj] = vv
            for hh in range(2):
                head = 2 * c + hh
                p0 = hh * 64
                for i in range(NT):
                    p = it % 2
                    it += 1
                    nkb = 2 * i + 2
                    col = i * 16 + head
                    z1s = zs1[:, col:col + 1]
                    z1b = zb1[:, col:col + 1]
                    z2s = zs2[:, col:col + 1]
                    z2b = zb2[:, col:col + 1]
                    done = 0
                    si = 0
                    while done < nkb:
                        take = min(8, nkb - done)
                        gj = gj_i % 2
                        gj_i += 1
                        def mm1(t, cj=cj, p0=p0, i=i, done=done, take=take):
                            inst = None
                            for s0 in range(0, take * 128, 512):
                                w = min(512, take * 128 - s0)
                                inst = t.matmul(
                                    psc1[:, s0:s0 + w],
                                    qtc[cj][p0:p0 + 64, i * 128:(i + 1) * 128],
                                    ktc[cj][p0:p0 + 64, done * 128 + s0:
                                            done * 128 + s0 + w],
                                    start=True, stop=True)
                            return inst
                        p1v = env.op("tensor", mm1,
                                     waits=[("dma_s", dq), ("dma_s", dk),
                                            ("act", p1_free if PIPE[0] >= 2
                                             else env.cur("act"))],
                                     inc="pe")
                        e1 = env.op("scalar",
                                    lambda s, take=take, z1s=z1s, z1b=z1b, gj=gj:
                                    s.activation(z1sb[gj][:, :take * 128],
                                                 psc1[:, :take * 128],
                                                 AF.Identity, bias=z1b,
                                                 scale=z1s),
                                    waits=[("pe", p1v), ("vec", z1_free[gj])],
                                    inc="act")
                        p1_free = e1
                        def mm2(t, cj=cj, p0=p0, i=i, done=done, take=take):
                            inst = None
                            for s0 in range(0, take * 128, 512):
                                w = min(512, take * 128 - s0)
                                inst = t.matmul(
                                    psc2[:, s0:s0 + w],
                                    q2c[cj][p0:p0 + 64, i * 128:(i + 1) * 128],
                                    k2c[cj][p0:p0 + 64, done * 128 + s0:
                                            done * 128 + s0 + w],
                                    start=True, stop=True)
                            return inst
                        p2v = env.op("tensor", mm2,
                                     waits=[("dma_s", dq2), ("dma_s", dk2),
                                            ("vec", p2_free)],
                                     inc="pe")
                        wv = env.op("vector",
                                    lambda v, take=take, gj=gj, z2s=z2s:
                                    v.scalar_tensor_tensor(
                                        wsb[gj][:, :take * 128],
                                        psc2[:, :take * 128], z2s,
                                        mixb[:, 1:2].broadcast_to(
                                            [128, take * 128]),
                                        ALU.mult, ALU.mult),
                                    waits=[("pe", p2v), ("vec", w_free[gj])],
                                    inc="vec")
                        p2_free = wv
                        sv = env.op("vector",
                                    lambda v, take=take, gj=gj, col=col:
                                    v.tensor_tensor(
                                        wsb[gj][:, :take * 128],
                                        wsb[gj][:, :take * 128],
                                        wcst[:, col:col + 1].broadcast_to(
                                            [128, take * 128]), ALU.add),
                                    waits=[("vec", wv)], inc="vec")
                        sv2 = env.op("vector",
                                     lambda v, take=take, done=done, gj=gj,
                                     p=p:
                                     v.tensor_tensor(
                                         s_sb[p][:, done * 128:
                                                 (done + take) * 128],
                                         z1sb[gj][:, :take * 128],
                                         wsb[gj][:, :take * 128], ALU.mult),
                                     waits=[("vec", sv), ("act", e1),
                                            ("act", ssb_free[p])], inc="vec")
                        z1_free[gj] = sv2
                        w_free[gj] = sv2
                        if done + take == nkb:
                            sv2 = env.op("vector",
                                         lambda v, p=p, off=(nkb - 2) * 128:
                                         v.tensor_tensor(
                                             s_sb[p][:, off:off + 256],
                                             s_sb[p][:, off:off + 256],
                                             dmask[:, :], ALU.add),
                                         waits=[("vec", sv2)], inc="vec")
                        env.op("vector",
                               lambda v, take=take, done=done, si=si, p=p:
                               v.tensor_reduce(
                                   maxc[p][:, si:si + 1],
                                   s_sb[p][:, done * 128:(done + take) * 128],
                                   AX.X, ALU.max),
                               waits=[("vec", sv2), ("vec", maxc_free[p])],
                               inc="vec")
                        done += take
                        si += 1
                    nm = env.op("vector",
                                lambda v, si=si, p=p:
                                v.tensor_reduce(negmax[p][:, :],
                                                maxc[p][:, :si],
                                                AX.X, ALU.max),
                                waits=[("vec", env.cur("vec")),
                                       ("act", nm_free[p])],
                                inc="vec")
                    maxc_free[p] = nm
                    nm2 = env.op("vector",
                                 lambda v, p=p:
                                 v.tensor_scalar(negmax[p][:, :],
                                                 negmax[p][:, :],
                                                 -1.0, 0.0, ALU.mult, ALU.add),
                                 waits=[("vec", nm)], inc="vec")
                    ne = (nkb + 7) // 8
                    lastexp = 0
                    for si2 in range(ne):
                        c0 = si2 * 1024
                        cw = min(1024, nkb * 128 - c0)
                        lastexp = env.op(
                            "scalar",
                            lambda s, c0=c0, cw=cw, si2=si2, p=p:
                            s.activation(a_sb[p][:, c0:c0 + cw],
                                         s_sb[p][:, c0:c0 + cw], AF.Exp,
                                         bias=negmax[p][:, :],
                                         accum_out=sums[p][:, si2:si2 + 1]),
                            waits=[("vec", nm2), ("vec", sums_free[p]),
                                   ("pe", asb_free[p] if PIPE[0] >= 2
                                    else env.cur("pe"))],
                            inc="act")
                    ssb_free[p] = lastexp
                    nm_free[p] = lastexp
                    rv = env.op("vector",
                                lambda v, ne=ne, p=p:
                                v.tensor_reduce(rinv[p][:, :], sums[p][:, :ne],
                                                AX.X, ALU.add),
                                waits=[("act", lastexp),
                                       ("vec", rinv_free[p])], inc="vec")
                    sums_free[p] = rv
                    rv2 = env.op("vector",
                                 lambda v, p=p:
                                 v.reciprocal(rinv[p][:, :], rinv[p][:, :]),
                                 waits=[("vec", rv)], inc="vec")
                    tv = 0
                    for g4 in range((nkb + 3) // 4):
                        jn = min(4, nkb - g4 * 4)
                        pj = g4 % 2
                        for j in range(jn):
                            kb = g4 * 4 + j
                            tv = env.op("tensor",
                                        lambda t, kb=kb, j=j, p=p, pj=pj:
                                        t.matmul(
                                            pat[pj][:, j * 128:(j + 1) * 128],
                                            a_sb[p][:, kb * 128:(kb + 1) * 128],
                                            ident[:, :], start=True, stop=True),
                                        waits=[("act", lastexp),
                                               ("vec", pat_free[pj])],
                                        inc="pe")
                        vv = env.op("vector",
                                    lambda v, g4=g4, jn=jn, p=p, pj=pj:
                                    v.tensor_copy(
                                        at_sb[p][:, g4 * 512:g4 * 512 + jn * 128],
                                        pat[pj][:, :jn * 128]),
                                    waits=[("pe", tv), ("pe", atsb_free[p])],
                                    inc="vec")
                        pat_free[pj] = vv
                    asb_free[p] = env.cur("pe")
                    aj = p
                    def av_mm(t, nkb=nkb, cj=cj, p0=p0, aj=aj, p=p):
                        vt = vtok[cj].rearrange("p (kb d) -> p kb d", d=128)
                        for kb in range(nkb):
                            inst = t.matmul(
                                pav[aj],
                                at_sb[p][:, kb * 128:(kb + 1) * 128],
                                vt[:, kb, p0:p0 + 64],
                                start=(kb == 0), stop=(kb == nkb - 1))
                        return inst
                    pav_v = env.op("tensor", av_mm,
                                   waits=[("vec", env.cur("vec")),
                                          ("vec", pav_free[aj]),
                                          ("act", 0 if PIPE[0] >= 2
                                           else env.cur("act"))],
                                   inc="pe")
                    atsb_free[p] = pav_v
                    last_avmm = pav_v
                    dst = attn_tok[:, i * H * 64 + head * 64:
                                   i * H * 64 + head * 64 + 64]
                    ev = env.op("vector",
                                lambda v, dst=dst, aj=aj, p=p:
                                v.tensor_scalar(dst, pav[aj], rinv[p][:, 0:1],
                                                None, ALU.mult),
                                waits=[("pe", pav_v), ("vec", rv2)], inc="vec")
                    pav_free[aj] = ev
                    rinv_free[p] = ev
            chunk_free[cj] = env.cur("act")
        env.flush("attn_quartet")


def emit_L0_attn(env, io):
    nc = env.nc
    qt_d = nc.dram_tensor("l0_qt", [D, TO], BF16, kind="Internal").ap()
    q2t_d = nc.dram_tensor("l0_q2t", [D, TO], BF16, kind="Internal").ap()
    kt_d = nc.dram_tensor("l0_kt", [D, T], BF16, kind="Internal").ap()
    k2t_d = nc.dram_tensor("l0_k2t", [D, T], BF16, kind="Internal").ap()
    vt_d = nc.dram_tensor("l0_vt", [D, T], BF16, kind="Internal").ap()

    mixb = sb(env, "l0mix", [128, 3], F32)
    qsb = sb(env, "l0qs", [128, 1], F32)
    env.begin()
    d1 = env.dma("sync", mixb[:, 0:1], io["L0_attn_mixb"])
    d2 = env.dma("sync", qsb[:, :], io["L0_attn_qsb"])
    s1 = env.op("scalar", lambda s: s.activation(mixb[:, 1:2], mixb[:, 0:1],
                                                 AF.Sigmoid),
                waits=[("dma_s", d1)], inc="act")
    s2 = env.op("vector", lambda v: v.tensor_scalar(
        mixb[:, 2:3], mixb[:, 1:2], -1.0, 1.0, ALU.mult, ALU.add),
        waits=[("act", s1)], inc="vec")
    env.op("vector", lambda v: v.tensor_tensor(
        mixb[:, 1:2], mixb[:, 1:2], qsb[:, :], ALU.mult),
        waits=[("vec", s2), ("dma_s", d2)], inc="vec")
    env.wait_all("sync", "dma_s")
    env.flush()

    with contextlib.ExitStack() as st:
        xt = st.enter_context(_sbuf(nc, "xtf", [128, DC * T], BF16))
        xto = st.enter_context(_sbuf(nc, "xto", [128, DC * TO], BF16))
        emit_build_xt(env, xt, io["x_ev"], io["x_od"], full=True)
        emit_build_xt(env, xto, io["x_own"], None, full=False)
        emit_proj_fm(env, [
            dict(wt=io["L0_attn_WqT"], n_in=D, n_out=D, out=qt_d, ntok=TO),
            dict(wt=io["L0_attn_Wq2T"], n_in=D, n_out=D, out=q2t_d, ntok=TO),
        ], xto, TO)
        emit_proj_fm(env, [
            dict(wt=io["L0_attn_WkT"], n_in=D, n_out=D, out=kt_d, ntok=T),
            dict(wt=io["L0_attn_Wk2T"], n_in=D, n_out=D, out=k2t_d, ntok=T),
            dict(wt=io["L0_attn_WvT"], n_in=D, n_out=D, out=vt_d, ntok=T),
        ], xt, T)

    with contextlib.ExitStack() as st:
        mu1 = st.enter_context(_sbuf(nc, "l0mu1", [128, NT * 16], F32))
        s21 = st.enter_context(_sbuf(nc, "l0s21", [128, NT * 16], F32))
        mu2 = st.enter_context(_sbuf(nc, "l0mu2", [128, NT * 16], F32))
        s22 = st.enter_context(_sbuf(nc, "l0s22", [128, NT * 16], F32))
        zs1 = st.enter_context(_sbuf(nc, "l0zs1", [128, NT * 16], F32))
        zb1 = st.enter_context(_sbuf(nc, "l0zb1", [128, NT * 16], F32))
        zs2 = st.enter_context(_sbuf(nc, "l0zs2", [128, NT * 16], F32))
        zb2 = st.enter_context(_sbuf(nc, "l0zb2", [128, NT * 16], F32))
        emit_l0_stats(env, qt_d, kt_d, mu1, s21)
        emit_l0_stats(env, q2t_d, k2t_d, mu2, s22)
        if DBG[0]:
            dmp = [nc.dram_tensor(f"dbg_z{i}", [128, NT * 16], F32,
                                  kind="ExternalOutput").ap() for i in range(4)]
            emit_l0_zparams(env, mu1, s21, zs1, zb1, dump=dmp[0:2])
            emit_l0_zparams(env, mu2, s22, zs2, zb2, dump=dmp[2:4])
        else:
            emit_l0_zparams(env, mu1, s21, zs1, zb1)
            emit_l0_zparams(env, mu2, s22, zs2, zb2)
        with contextlib.ExitStack() as st2:
            attn_tok = st2.enter_context(
                _sbuf(nc, "attn_tok", [128, NT * H * 64], BF16))
            attn_fm = st2.enter_context(
                _sbuf(nc, "attn_fm", [128, DC * TO], BF16))
            wcst = st2.enter_context(_sbuf(nc, "l0wcst", [128, NT * 16], F32))
            env.begin()
            env.op("vector", lambda v: v.scalar_tensor_tensor(
                wcst[:, :], zb2[:, :], mixb[:, 1:2],
                mixb[:, 2:3].broadcast_to([128, NT * 16]),
                ALU.mult, ALU.add), inc="vec")
            env.flush("wcst")
            emit_attn_quartet(env, qt_d, q2t_d, kt_d, k2t_d, vt_d,
                              zs1, zb1, zs2, zb2, mixb, qsb, attn_tok, wcst)
            emit_tok2fm(env, attn_tok, attn_fm, DC)
            emit_tok_proj_resid_ln(env, attn_fm, D, io["L0_attn_WoT"],
                                   io["x_own"], io["x_out"])


def emit_L2_ffn(env, io):
    nc = env.nc
    h1_d = nc.dram_tensor("l2f_h1", [256, TO], BF16, kind="Internal").ap()

    knotb = sb(env, "l2f_knot", [128, 8], F32)
    rtb = sb(env, "l2f_rt", [128, 1], F32)
    cb1 = sb(env, "l2f_cb1", [128, 2], F32)
    cb2 = sb(env, "l2f_cb2", [2, 1], F32)
    w1b = sb(env, "l2f_w1", [2, 1024], BF16)
    b1bc = sb(env, "l2f_b1", [128, 1024], F32)
    w2bc = sb(env, "l2f_w2", [128, 1024], BF16)
    b2bc = sb(env, "l2f_b2", [128, 64], F32)
    dirs = sb(env, "l2f_dir", [64, 1024], BF16)
    gbc = sb(env, "l2f_g", [128, D], F32)
    bbc = sb(env, "l2f_b", [128, D], F32)
    osb = sb(env, "l2f_os", [128, 1], F32)
    sigsT = sb(env, "l2f_sig", [128, 8 * 64], BF16)
    csT = sb(env, "l2f_cs", [128, 8 * 8], BF16)
    env.begin()
    for tile, key in ((knotb, "L2f_knotb"), (rtb, "L2f_tempb"),
                      (cb1, "L2f_cb1c"), (b1bc, "L2f_b1bc"),
                      (b2bc, "L2f_b2bc"), (gbc, "L2f_gbc"),
                      (bbc, "L2f_bbc"), (osb, "L2f_osb"), (cb2, "L2f_cb2c")):
        env.dma("sync", tile[:, :], io[key])
    for tile, key in ((w1b, "L2f_w1all"), (w2bc, "L2f_w2bc"),
                      (dirs, "L2f_dirs")):
        env.dma("gpsimd", tile[:, :], io[key])
    r1 = env.op("vector", lambda v: v.tensor_scalar(
        rtb[:, :], rtb[:, :], 1.0, 1e-6, ALU.mult, ALU.add),
        waits=[("dma_s", env.cur("dma_s"))], inc="vec")
    env.op("vector", lambda v: v.reciprocal(rtb[:, :], rtb[:, :]),
           waits=[("vec", r1)], inc="vec")
    env.wait_all("sync", "dma_s", "dma_g")
    env.wait_all("gpsimd", "dma_s", "dma_g")
    env.flush()

    with contextlib.ExitStack() as st:
        srb = st.enter_context(_sbuf(nc, "l2f_sr", [128, 64], F32))
        qa = st.enter_context(_sbuf(nc, "l2f_qa", [128, 64], F32))
        qb = st.enter_context(_sbuf(nc, "l2f_qb", [128, 64], F32))
        cm = st.enter_context(_sbuf(nc, "l2f_cm", [128, 8], F32))
        env.begin()
        for c in range(8):
            dv = env.dma("sync", srb[:, :], io["L2f_srT"][c * 128:(c + 1) * 128, :],
                         waits=[("vec", env.cur("vec")), ("act", env.cur("act"))])
            a1 = env.op("vector", lambda v: v.tensor_scalar(
                qa[:, :], srb[:, :], 0.3, 0.0, ALU.is_gt, ALU.add),
                waits=[("dma_s", dv)], inc="vec")
            a2 = env.op("vector", lambda v: v.tensor_scalar(
                qb[:, :], srb[:, :], -0.3, 0.0, ALU.is_lt, ALU.add),
                waits=[("dma_s", dv)], inc="vec")
            a3 = env.op("vector", lambda v, c=c: v.tensor_tensor(
                sigsT[:, c * 64:(c + 1) * 64], qa[:, :], qb[:, :], ALU.subtract),
                waits=[("vec", a2)], inc="vec")
            a4 = env.op("vector", lambda v, c=c: v.tensor_reduce(
                cm[:, :], sigsT[:, c * 64:(c + 1) * 64].rearrange(
                    "p (a b) -> p a b", b=8), AX.X, ALU.add),
                waits=[("vec", a3)], inc="vec")
            a5 = env.op("scalar", lambda s, c=c: s.sign(
                csT[:, c * 8:(c + 1) * 8], cm[:, :]),
                waits=[("vec", a4)], inc="act")
        env.flush("l2f_sigs")

    comp = sb(env, "l2f_comp", [2, TO], BF16)
    with contextlib.ExitStack() as st:
        xto = st.enter_context(_sbuf(nc, "xto", [128, DC * TO], BF16))
        emit_build_xt(env, xto, io["x_own"], None, full=False)
        emit_proj_fm(env, [
            dict(wt=io["L2f_cw1"], n_in=D, n_out=256, out=h1_d, ntok=TO,
                 bias=cb1, act=AF.Gelu),
        ], xto, TO)

        with contextlib.ExitStack() as st2:
            h1s = st2.enter_context(_sbuf(nc, "l2f_h1s", [128, 2 * TO], BF16))
            cw2s = st2.enter_context(_sbuf(nc, "l2f_cw2", [128, 4], BF16))
            pc2 = st2.enter_context(_psum(nc, "l2f_pc2", [2, 512], F32))
            env.begin()
            d1 = env.dma("sync", h1s[:, :].rearrange("p (c t) -> p c t", c=2),
                         h1_d.rearrange("(c p) t -> p c t", p=128))
            d2 = env.dma("gpsimd", cw2s[:, :].rearrange("p (c o) -> p c o", o=2),
                         io["L2f_cw2"].rearrange("(c p) o -> p c o", p=128))
            pv = 0
            for g in range(4):
                def mm(t, g=g):
                    for kc in range(2):
                        inst = t.matmul(pc2[:, :], cw2s[:, kc * 2:kc * 2 + 2],
                                        h1s[:, kc * TO + g * 512:
                                            kc * TO + g * 512 + 512],
                                        start=(kc == 0), stop=(kc == 1))
                    return inst
                pv = env.op("tensor", mm,
                            waits=[("dma_s", d1), ("dma_g", d2),
                                   ("act", env.cur("act"))], inc="pe")
                env.op("scalar",
                       lambda s, g=g:
                       s.activation(comp[0:2, g * 512:(g + 1) * 512],
                                    pc2[:, :], AF.Tanh, bias=cb2[:, :]),
                       waits=[("pe", pv)], inc="act")
            env.flush("l2f_comp")

        emit_l2f_main(env, xto, comp, sigsT, csT, knotb, rtb, w1b, b1bc, w2bc,
                      b2bc, dirs, gbc, bbc, osb, io)


def _calibrate_gates(env, out_sb, scores_ap, n, knotb, rtb, scratch):
    nrm, idxf, ili, ilf, acc, acc2 = scratch
    a = env.op("scalar", lambda s: s.activation(
        nrm[:, :n], scores_ap, AF.Sigmoid, scale=rtb[:, 0:1]),
        waits=[("pe", env.cur("pe")), ("vec", env.cur("vec"))], inc="act")
    b = env.op("vector", lambda v: v.tensor_scalar(
        idxf[:, :n], nrm[:, :n], 7.0, 0.0, ALU.mult, ALU.add),
        waits=[("act", a)], inc="vec")
    c = env.op("vector", lambda v: v.tensor_scalar(
        ili[:, :n], idxf[:, :n], 1.0, -0.5, ALU.mult, ALU.add),
        waits=[("vec", b)], inc="vec")
    d = env.op("vector", lambda v: v.tensor_copy(ilf[:, :n], ili[:, :n]),
               waits=[("vec", c)], inc="vec")
    e = env.op("vector", lambda v: v.tensor_scalar(
        ilf[:, :n], ilf[:, :n], 6.0, 0.0, ALU.min, ALU.max),
        waits=[("vec", d)], inc="vec")
    f = env.op("vector", lambda v: v.memset(acc[:, :n], 0.0),
               waits=[("vec", env.cur("vec")), ("act", env.cur("act"))],
               inc="vec")
    g = env.op("vector", lambda v: v.memset(acc2[:, :n], 0.0),
               waits=[("vec", f)], inc="vec")
    last = g
    for j in range(7):
        m1 = env.op("vector", lambda v, j=j: v.tensor_scalar(
            nrm[:, :n], ilf[:, :n], float(j), 0.0, ALU.is_equal, ALU.add),
            waits=[("vec", last)], inc="vec")
        m2 = env.op("vector", lambda v, j=j: v.scalar_tensor_tensor(
            acc[:, :n], nrm[:, :n], knotb[:, j:j + 1], acc[:, :n],
            ALU.mult, ALU.add), waits=[("vec", m1)], inc="vec")
        last = env.op("vector", lambda v, j=j: v.scalar_tensor_tensor(
            acc2[:, :n], nrm[:, :n], knotb[:, j + 1:j + 2], acc2[:, :n],
            ALU.mult, ALU.add), waits=[("vec", m2)], inc="vec")
    h = env.op("vector", lambda v: v.tensor_tensor(
        idxf[:, :n], idxf[:, :n], ilf[:, :n], ALU.subtract),
        waits=[("vec", last)], inc="vec")
    i1 = env.op("vector", lambda v: v.tensor_tensor(
        acc2[:, :n], acc2[:, :n], acc[:, :n], ALU.subtract),
        waits=[("vec", h)], inc="vec")
    i2 = env.op("vector", lambda v: v.tensor_tensor(
        acc2[:, :n], acc2[:, :n], idxf[:, :n], ALU.mult),
        waits=[("vec", i1)], inc="vec")
    return env.op("vector", lambda v: v.tensor_tensor(
        out_sb[:, :n], acc[:, :n], acc2[:, :n], ALU.add),
        waits=[("vec", i2)], inc="vec")


def emit_l2f_main(env, xto, comp, sigsT, csT, knotb, rtb, w1b, b1bc, w2bc,
                  b2bc, dirs, gbc, bbc, osb, io):
    nc = env.nc
    with contextlib.ExitStack() as st:
        xbuf = [st.enter_context(_sbuf(nc, f"xr{i}", [128, D], F32))
                for i in range(2)]
        xn = [st.enter_context(_sbuf(nc, f"xn{i}", [128, D], F32))
              for i in range(2)]
        y3 = st.enter_context(_sbuf(nc, "l2f_y3", [128, D], F32))
        stats = st.enter_context(_sbuf(nc, "stats", [128, 2 * NT], F32))
        sumsq = st.enter_context(_sbuf(nc, "sumsq", [128, 2 * NT], F32))
        stats2 = st.enter_context(_sbuf(nc, "stats2", [128, 2 * NT], F32))
        sumsq2 = st.enter_context(_sbuf(nc, "sumsq2", [128, 2 * NT], F32))
        mean = st.enter_context(_sbuf(nc, "mean", [128, NT], F32))
        rstd = st.enter_context(_sbuf(nc, "rstd", [128, NT], F32))
        nmr = st.enter_context(_sbuf(nc, "nmr", [128, NT], F32))
        sq = st.enter_context(_sbuf(nc, "sqs", [128, 512], BF16))
        tg = st.enter_context(_sbuf(nc, "l2f_tg", [128, 64], F32))
        cg = st.enter_context(_sbuf(nc, "l2f_cg", [128, 8], F32))
        sel = st.enter_context(_sbuf(nc, "l2f_sel", [128, 8], F32))
        mx8 = st.enter_context(_sbuf(nc, "l2f_mx8", [128, 8], F32))
        ix8 = st.enter_context(_sbuf(nc, "l2f_ix8", [128, 8], U32))
        cif = st.enter_context(_sbuf(nc, "l2f_ci", [128, 1], F32))
        lif = st.enter_context(_sbuf(nc, "l2f_li", [128, 1], F32))
        tidx = st.enter_context(_sbuf(nc, "l2f_ti", [128, 1], F32))
        hh_sb = st.enter_context(_sbuf(nc, "l2f_hh", [128, 1024], BF16))
        mag = st.enter_context(_sbuf(nc, "l2f_mag", [128, 64], F32))
        selm = st.enter_context(_sbuf(nc, "l2f_slm", [128, 64], BF16))
        selmT = st.enter_context(_sbuf(nc, "l2f_slt", [64, 128], BF16))
        s1 = st.enter_context(_sbuf(nc, "l2f_s1", [128, 64], F32))
        s2 = st.enter_context(_sbuf(nc, "l2f_s2", [128, 64], F32))
        s3 = st.enter_context(_sbuf(nc, "l2f_s3", [128, 64], I32))
        s4 = st.enter_context(_sbuf(nc, "l2f_s4", [128, 64], F32))
        s5 = st.enter_context(_sbuf(nc, "l2f_s5", [128, 64], F32))
        s6 = st.enter_context(_sbuf(nc, "l2f_s6", [128, 64], F32))
        scratch = (s1, s2, s3, s4, s5, s6)
        psc = st.enter_context(_psum(nc, "l2f_ps", [128, 64], F32))
        pcc = st.enter_context(_psum(nc, "l2f_pc", [128, 8], F32))
        psp = [st.enter_context(_psum(nc, f"l2f_pp{i}", [128, 512], F32))
               for i in range(2)]
        pdr = [st.enter_context(_psum(nc, f"l2f_pd{i}", [128, 512], F32))
               for i in range(2)]
        pat = st.enter_context(_psum(nc, "l2f_pat", [64, 128], F32))
        env.begin()
        ident = env.consts["c_ident"]
        iotab = env.consts["c_iota64"]
        xfree = [0, 0]
        sqfree = 0
        for i in range(NT):
            bj = i % 2
            dx = env.dma("sync", xbuf[bj][:, :],
                         io["x_own"][i * 128:(i + 1) * 128, :],
                         waits=[("out", xfree[bj])])
            def mm_t(t, i=i):
                for kc in range(DC):
                    inst = t.matmul(psc[:, :],
                                    xto[:, kc * TO + i * 128: kc * TO + (i + 1) * 128],
                                    sigsT[:, kc * 64:(kc + 1) * 64],
                                    start=(kc == 0), stop=(kc == DC - 1))
                return inst
            p1 = env.op("tensor", mm_t, waits=[("act", env.cur("act")),
                                               ("vec", env.cur("vec"))],
                        inc="pe")
            gv1 = _calibrate_gates(env, tg, psc[:, :], 64, knotb, rtb, scratch)
            def mm_c(t, i=i):
                for kc in range(DC):
                    inst = t.matmul(pcc[:, :],
                                    xto[:, kc * TO + i * 128: kc * TO + (i + 1) * 128],
                                    csT[:, kc * 8:(kc + 1) * 8],
                                    start=(kc == 0), stop=(kc == DC - 1))
                return inst
            p2 = env.op("tensor", mm_c, waits=[("act", env.cur("act")),
                                               ("vec", gv1)], inc="pe")
            gv2 = _calibrate_gates(env, cg, pcc[:, :], 8, knotb, rtb, scratch)
            c1 = env.op("vector", lambda v: v.max(mx8[:, :], cg[:, :]),
                        waits=[("vec", gv2)], inc="vec")
            c2 = env.op("vector", lambda v: v.max_index(ix8[:, :], mx8[:, :],
                                                        cg[:, :]),
                        waits=[("vec", c1)], inc="vec")
            c3 = env.op("vector", lambda v: v.tensor_copy(cif[:, :],
                                                          ix8[:, 0:1]),
                        waits=[("vec", c2)], inc="vec")
            c4 = env.op("vector", lambda v: v.memset(sel[:, :], 0.0),
                        waits=[("vec", env.cur("vec"))], inc="vec")
            last = c4
            for cc in range(8):
                m1 = env.op("vector", lambda v, cc=cc: v.tensor_scalar(
                    s1[:, 0:1], cif[:, :], float(cc), 0.0, ALU.is_equal,
                    ALU.add), waits=[("vec", last)], inc="vec")
                last = env.op("vector", lambda v, cc=cc: v.scalar_tensor_tensor(
                    sel[:, :], tg[:, cc * 8:(cc + 1) * 8], s1[:, 0:1],
                    sel[:, :], ALU.mult, ALU.add),
                    waits=[("vec", m1)], inc="vec")
            c5 = env.op("vector", lambda v: v.max(mx8[:, :], sel[:, :]),
                        waits=[("vec", last)], inc="vec")
            c6 = env.op("vector", lambda v: v.max_index(ix8[:, :], mx8[:, :],
                                                        sel[:, :]),
                        waits=[("vec", c5)], inc="vec")
            c7 = env.op("vector", lambda v: v.tensor_copy(lif[:, :],
                                                          ix8[:, 0:1]),
                        waits=[("vec", c6)], inc="vec")
            c8 = env.op("vector", lambda v: v.scalar_tensor_tensor(
                tidx[:, :], cif[:, :], 8.0, lif[:, :], ALU.mult, ALU.add),
                waits=[("vec", c7)], inc="vec")
            for g in range(2):
                pv = env.op("tensor",
                            lambda t, i=i, g=g:
                            t.matmul(psp[g][:, :],
                                     comp[0:2, i * 128:(i + 1) * 128],
                                     w1b[0:2, g * 512:(g + 1) * 512],
                                     start=True, stop=True),
                            waits=[("vec", env.cur("vec")),
                                   ("act", env.cur("act"))], inc="pe")
                vv = env.op("vector",
                            lambda v, g=g:
                            v.tensor_tensor(psp[g][:, :], psp[g][:, :],
                                            b1bc[:, g * 512:(g + 1) * 512],
                                            ALU.add),
                            waits=[("pe", pv)], inc="vec")
                env.op("scalar",
                       lambda s, g=g:
                       s.activation(hh_sb[:, g * 512:(g + 1) * 512],
                                    psp[g][:, :], AF.Relu),
                       waits=[("vec", vv)], inc="act")
            m1 = env.op("vector", lambda v: v.tensor_tensor(
                hh_sb[:, :], hh_sb[:, :], w2bc[:, :], ALU.mult),
                waits=[("act", env.cur("act"))], inc="vec")
            m2 = env.op("vector", lambda v: v.tensor_reduce(
                mag[:, :], hh_sb[:, :].rearrange("p (a b) -> p a b", b=16),
                AX.X, ALU.add), waits=[("vec", m1)], inc="vec")
            m3 = env.op("vector", lambda v: v.tensor_tensor(
                mag[:, :], mag[:, :], b2bc[:, :], ALU.add),
                waits=[("vec", m2)], inc="vec")
            m4 = env.op("vector", lambda v: v.tensor_scalar(
                s1[:, :], iotab[:, :], tidx[:, 0:1], 0.0, ALU.is_equal,
                ALU.add), waits=[("vec", m3)], inc="vec")
            m5 = env.op("vector", lambda v: v.tensor_tensor(
                selm[:, :], mag[:, :], s1[:, :], ALU.mult),
                waits=[("vec", m4)], inc="vec")
            tv = env.op("tensor", lambda t: t.matmul(
                pat[:, :], selm[:, :], ident[:, :], start=True, stop=True),
                waits=[("vec", m5)], inc="pe")
            te = env.op("vector", lambda v: v.tensor_copy(selmT[:, :], pat[:, :]),
                        waits=[("pe", tv)], inc="vec")
            for g in range(2):
                pv = env.op("tensor",
                            lambda t, g=g:
                            t.matmul(pdr[g][:, :], selmT[:, :],
                                     dirs[:, g * 512:(g + 1) * 512],
                                     start=True, stop=True),
                            waits=[("vec", te)], inc="pe")
                vv = env.op("vector",
                            lambda v, g=g, bj=bj, i=i:
                            v.scalar_tensor_tensor(
                                xn[bj][:, g * 512:(g + 1) * 512],
                                pdr[g][:, :], osb[:, 0:1],
                                xbuf[bj][:, g * 512:(g + 1) * 512],
                                ALU.mult, ALU.add,
                                accum_out=stats[:, 2 * i + g:2 * i + g + 1]),
                            waits=[("pe", pv), ("dma_s", dx),
                                   ("out", xfree[bj])], inc="vec")
                sv = env.op("scalar",
                            lambda s, bj=bj, i=i, g=g:
                            s.activation(sq[:, :],
                                         xn[bj][:, g * 512:(g + 1) * 512],
                                         AF.Square,
                                         accum_out=sumsq[:, 2 * i + g:
                                                         2 * i + g + 1]),
                            waits=[("vec", vv), ("act", sqfree)], inc="act")
                sqfree = sv
            ap1 = _ln_tail(env, i, bj, xn, stats, sumsq, mean, rstd, nmr, 1e-5)
            g1 = env.op("vector", lambda v, bj=bj: v.tensor_tensor(
                y3[:, :], xn[bj][:, :], gbc[:, :], ALU.mult),
                waits=[("act", ap1)], inc="vec")
            g2 = env.op("vector", lambda v, bj=bj: v.tensor_tensor(
                y3[:, :], y3[:, :], bbc[:, :], ALU.add),
                waits=[("vec", g1)], inc="vec")
            for g in range(2):
                vv = env.op("vector",
                            lambda v, g=g, bj=bj, i=i:
                            v.scalar_tensor_tensor(
                                xn[bj][:, g * 512:(g + 1) * 512],
                                y3[:, g * 512:(g + 1) * 512], 1.0,
                                xbuf[bj][:, g * 512:(g + 1) * 512],
                                ALU.mult, ALU.add,
                                accum_out=stats2[:, 2 * i + g:2 * i + g + 1]),
                            waits=[("vec", g2)], inc="vec")
                sv = env.op("scalar",
                            lambda s, bj=bj, i=i, g=g:
                            s.activation(sq[:, :],
                                         xn[bj][:, g * 512:(g + 1) * 512],
                                         AF.Square,
                                         accum_out=sumsq2[:, 2 * i + g:
                                                          2 * i + g + 1]),
                            waits=[("vec", vv), ("act", sqfree)], inc="act")
                sqfree = sv
            ap2 = _ln_tail(env, i, bj, xn, stats2, sumsq2, mean, rstd, nmr,
                           1e-5)
            xfree[bj] = env.op("sync",
                               lambda e, bj=bj, i=i:
                               e.dma_start(
                                   out=io["x_out"][i * 128:(i + 1) * 128, :],
                                   in_=xn[bj][:, :]),
                               waits=[("act", ap2)], inc="out", k=16)
        env.wait_all("sync", "out")
        env.flush("l2f_main")


FP4_CLOW = 6291456.0
FP4_VELT = 4194305.0


def emit_fp4q(env, out_ap, in_ap, W, scr, waits=()):
    y, h_, d_, m_, sc, rc = (scr[k] for k in ("y", "h", "d", "m", "sc", "rc"))
    G = W // 16
    a1 = env.op("vector", lambda v: v.tensor_reduce(
        sc[:, :G], in_ap.rearrange("p (g b) -> p g b", b=16), AX.X, ALU.max,
        apply_absolute_value=True), waits=waits, inc="vec")
    a2 = env.op("vector", lambda v: v.tensor_scalar(
        sc[:, :G], sc[:, :G], 1.0 / 6.0, 1e-12, ALU.mult, ALU.max),
        waits=[("vec", a1)], inc="vec")
    a3 = env.op("vector", lambda v: v.reciprocal(rc[:, :G], sc[:, :G]),
                waits=[("vec", a2)], inc="vec")
    a4 = env.op("vector", lambda v: v.tensor_tensor(
        y[:, :W].rearrange("p (g b) -> p g b", b=16),
        in_ap.rearrange("p (g b) -> p g b", b=16),
        rc[:, :G].unsqueeze(2).broadcast_to([128, G, 16]),
        ALU.mult), waits=[("vec", a3)], inc="vec")
    a5 = env.op("vector", lambda v: v.tensor_scalar(
        d_[:, :W], y[:, :W], FP4_CLOW, 0.0, ALU.add, ALU.add),
        waits=[("vec", a4)], inc="vec")
    a6 = env.op("vector", lambda v: v.tensor_scalar(
        d_[:, :W], d_[:, :W], -FP4_CLOW, 0.0, ALU.add, ALU.add),
        waits=[("vec", a5)], inc="vec")
    a7 = env.op("vector", lambda v: v.tensor_scalar(
        h_[:, :W], y[:, :W], FP4_VELT, 0.0, ALU.mult, ALU.add),
        waits=[("vec", a4)], inc="vec")
    a8 = env.op("vector", lambda v: v.tensor_tensor(
        m_[:, :W], h_[:, :W], y[:, :W], ALU.subtract),
        waits=[("vec", a7)], inc="vec")
    a9 = env.op("vector", lambda v: v.tensor_tensor(
        h_[:, :W], h_[:, :W], m_[:, :W], ALU.subtract),
        waits=[("vec", a8)], inc="vec")
    am0 = env.op("vector", lambda v: v.tensor_tensor(
        m_[:, :W], y[:, :W], y[:, :W], ALU.mult),
        waits=[("vec", a9)], inc="vec")
    am = env.op("vector", lambda v: v.tensor_scalar(
        scr["u8"][:, :W], m_[:, :W], 5.0625, None, ALU.is_lt),
        waits=[("vec", am0)], inc="vec")
    asel = env.op("vector", lambda v: v.select(
        y[:, :W], scr["u8"][:, :W], d_[:, :W], h_[:, :W]),
        waits=[("vec", am)], inc="vec")
    return env.op("vector", lambda v: v.tensor_tensor(
        out_ap.rearrange("p (g b) -> p g b", b=16),
        y[:, :W].rearrange("p (g b) -> p g b", b=16),
        sc[:, :G].unsqueeze(2).broadcast_to([128, G, 16]),
        ALU.mult), waits=[("vec", asel)], inc="vec")


def _mk_fp4_scratch(env, st, W):
    nc = env.nc
    return dict(
        y=st.enter_context(_sbuf(nc, "fq_y", [128, W], F32)),
        h=st.enter_context(_sbuf(nc, "fq_h", [128, W], F32)),
        d=st.enter_context(_sbuf(nc, "fq_d", [128, W], F32)),
        m=st.enter_context(_sbuf(nc, "fq_m", [128, W], F32)),
        u8=st.enter_context(_sbuf(nc, "fq_u", [128, W], U8)),
        sc=st.enter_context(_sbuf(nc, "fq_s", [128, W // 16], F32)),
        rc=st.enter_context(_sbuf(nc, "fq_r", [128, W // 16], F32)),
    )


def emit_l1f_wquant(env, io, wgT_d, wuT_d, wdT_d):
    nc = env.nc
    specs = [(io["L1_ffn_w_gate"], 2816, 1024, wgT_d),
             (io["L1_ffn_w_up"], 2816, 1024, wuT_d),
             (io["L1_ffn_w_down"], 1024, 2816, wdT_d)]
    with contextlib.ExitStack() as st:
        win = st.enter_context(_sbuf(nc, "wq_in", [128, 2816], F32))
        qo = st.enter_context(_sbuf(nc, "wq_q", [128, 2816], BF16))
        stg = st.enter_context(_sbuf(nc, "wq_st", [128, 2816], BF16))
        scr = _mk_fp4_scratch(env, st, 1408)
        pt = st.enter_context(_psum(nc, "wq_pt", [128, 512], F32))
        env.begin()
        ident = env.consts["c_ident"]
        prev_store = 0
        prev_q = 0
        for (wd, nrow, ncol, outT) in specs:
            outT_r = outT.rearrange("(c p) o -> p c o", p=128)
            for r in range(nrow // 128):
                dv = env.dma("sync", win[:, :ncol], wd[r * 128:(r + 1) * 128, :],
                             waits=[("vec", prev_q)])
                qv = 0
                for s0 in range(0, ncol, 1408):
                    sw = min(1408, ncol - s0)
                    qv = emit_fp4q(env, qo[:, s0:s0 + sw], win[:, s0:s0 + sw],
                                   sw, scr,
                                   waits=[("dma_s", dv), ("pe", env.cur("pe"))])
                prev_q = qv
                for g4 in range((ncol // 128 + 3) // 4):
                    jn = min(4, ncol // 128 - g4 * 4)
                    tv = 0
                    for j in range(jn):
                        c = g4 * 4 + j
                        tv = env.op("tensor",
                                    lambda t, c=c, j=j:
                                    t.matmul(pt[:, j * 128:(j + 1) * 128],
                                                qo[:, c * 128:(c + 1) * 128],
                                                ident[:, :], start=True, stop=True),
                                    waits=[("vec", qv), ("vec", env.cur("vec")),
                                           ("dma_s", prev_store)],
                                    inc="pe")
                    env.op("vector",
                           lambda v, g4=g4, jn=jn:
                           v.tensor_copy(stg[:, g4 * 512:g4 * 512 + jn * 128],
                                         pt[:, :jn * 128]),
                           waits=[("pe", tv)], inc="vec")
                prev_store = env.dma(
                    "sync", outT_r[:, :, r * 128:(r + 1) * 128],
                    stg[:, :ncol].rearrange("p (c o) -> p c o", o=128),
                    waits=[("vec", env.cur("vec"))])
        env.wait_all("sync", "dma_s")
        env.flush("l1f_wquant")


def emit_l1f_xquant(env, io, xq_d):
    nc = env.nc
    with contextlib.ExitStack() as st:
        xin = st.enter_context(_sbuf(nc, "xq_in", [128, D], F32))
        qo = st.enter_context(_sbuf(nc, "xq_q", [128, D], BF16))
        scr = _mk_fp4_scratch(env, st, D)
        env.begin()
        prev_store = 0
        prev_q = 0
        for i in range(NT):
            dv = env.dma("sync", xin[:, :], io["x_own"][i * 128:(i + 1) * 128, :],
                         waits=[("vec", prev_q)])
            qv = emit_fp4q(env, qo[:, :], xin[:, :], D, scr,
                           waits=[("dma_s", dv), ("dma_s", prev_store)])
            prev_q = qv
            prev_store = env.dma("sync", xq_d[i * 128:(i + 1) * 128, :],
                                 qo[:, :], waits=[("vec", qv)])
        env.wait_all("sync", "dma_s")
        env.flush("l1f_xquant")


def emit_l1f_gateup(env, wgT_d, wuT_d, xq_fm, h_fm):
    nc = env.nc
    with contextlib.ExitStack() as st:
        wg = [st.enter_context(_sbuf(nc, f"gu_wg{i}", [128, D], BF16))
              for i in range(2)]
        wu = [st.enter_context(_sbuf(nc, f"gu_wu{i}", [128, D], BF16))
              for i in range(2)]
        sg = st.enter_context(_sbuf(nc, "gu_sg", [128, 512], BF16))
        pg = [st.enter_context(_psum(nc, f"gu_pg{i}", [128, 512], F32))
              for i in range(2)]
        pu = [st.enter_context(_psum(nc, f"gu_pu{i}", [128, 512], F32))
              for i in range(2)]
        env.begin()
        wfree = [0, 0]
        pgf = [0, 0]
        puf = [0, 0]
        for m in range(22):
            bj = m % 2
            d1 = env.dma("gpsimd",
                         wg[bj][:, :].rearrange("p (kc o) -> p kc o", o=128),
                         wgT_d[:, m * 128:(m + 1) * 128].rearrange(
                             "(kc p) o -> p kc o", p=128),
                         waits=[("pe", wfree[bj])])
            d2 = env.dma("gpsimd",
                         wu[bj][:, :].rearrange("p (kc o) -> p kc o", o=128),
                         wuT_d[:, m * 128:(m + 1) * 128].rearrange(
                             "(kc p) o -> p kc o", p=128))
            for g in range(4):
                pj = g % 2
                def mmg(t, bj=bj, g=g, pj=pj):
                    for kc in range(DC):
                        inst = t.matmul(pg[pj][:, :],
                                        wg[bj][:, kc * 128:(kc + 1) * 128],
                                        xq_fm[:, kc * TO + g * 512:
                                              kc * TO + g * 512 + 512],
                                        start=(kc == 0), stop=(kc == DC - 1))
                    return inst
                g1 = env.op("tensor", mmg,
                            waits=[("dma_g", d1), ("vec", pgf[pj])], inc="pe")
                def mmu(t, bj=bj, g=g, pj=pj):
                    for kc in range(DC):
                        inst = t.matmul(pu[pj][:, :],
                                        wu[bj][:, kc * 128:(kc + 1) * 128],
                                        xq_fm[:, kc * TO + g * 512:
                                              kc * TO + g * 512 + 512],
                                        start=(kc == 0), stop=(kc == DC - 1))
                    return inst
                g2 = env.op("tensor", mmu,
                            waits=[("dma_g", d2), ("vec", puf[pj])], inc="pe")
                a1 = env.op("scalar",
                            lambda s, pj=pj: s.activation(sg[:, :], pg[pj][:, :],
                                                          AF.Silu),
                            waits=[("pe", g1), ("vec", env.cur("vec"))],
                            inc="act")
                vv = env.op("vector",
                            lambda v, pj=pj, m=m, g=g:
                            v.tensor_tensor(
                                h_fm[:, m * TO + g * 512: m * TO + g * 512 + 512],
                                sg[:, :], pu[pj][:, :], ALU.mult),
                            waits=[("act", a1), ("pe", g2)], inc="vec")
                pgf[pj] = vv
                puf[pj] = vv
            wfree[bj] = env.cur("pe")
        env.flush("l1f_gateup")


def emit_l1f_hquant(env, h_fm):
    nc = env.nc
    I = 2816
    IC = 22
    with contextlib.ExitStack() as st:
        htok = st.enter_context(_sbuf(nc, "hq_t", [128, I], F32))
        hq = st.enter_context(_sbuf(nc, "hq_q", [128, I], BF16))
        scr = _mk_fp4_scratch(env, st, 1408)
        pt = st.enter_context(_psum(nc, "hq_pt", [128, 512], F32))
        env.begin()
        ident = env.consts["c_ident"]
        for i in range(NT):
            for g4 in range((IC + 3) // 4):
                jn = min(4, IC - g4 * 4)
                tv = 0
                for j in range(jn):
                    c = g4 * 4 + j
                    tv = env.op("tensor",
                                lambda t, c=c, j=j, i=i:
                                t.matmul(pt[:, j * 128:(j + 1) * 128],
                                            h_fm[:, c * TO + i * 128:
                                                 c * TO + (i + 1) * 128],
                                            ident[:, :], start=True, stop=True),
                                waits=[("vec", env.cur("vec"))], inc="pe")
                env.op("vector",
                       lambda v, g4=g4, jn=jn:
                       v.tensor_copy(htok[:, g4 * 512:g4 * 512 + jn * 128],
                                     pt[:, :jn * 128]),
                       waits=[("pe", tv)], inc="vec")
            qv = 0
            for s0 in (0, 1408):
                qv = emit_fp4q(env, hq[:, s0:s0 + 1408], htok[:, s0:s0 + 1408],
                               1408, scr, waits=[("vec", env.cur("vec"))])
            for g4 in range((IC + 3) // 4):
                jn = min(4, IC - g4 * 4)
                tv = 0
                for j in range(jn):
                    c = g4 * 4 + j
                    tv = env.op("tensor",
                                lambda t, c=c, j=j:
                                t.matmul(pt[:, j * 128:(j + 1) * 128],
                                            hq[:, c * 128:(c + 1) * 128],
                                            ident[:, :], start=True, stop=True),
                                waits=[("vec", qv), ("vec", env.cur("vec"))],
                                inc="pe")
                for j in range(jn):
                    c = g4 * 4 + j
                    env.op("vector",
                           lambda v, c=c, j=j, i=i:
                           v.tensor_copy(h_fm[:, c * TO + i * 128:
                                              c * TO + (i + 1) * 128],
                                         pt[:, j * 128:(j + 1) * 128]),
                           waits=[("pe", tv)], inc="vec")
        env.flush("l1f_hquant")


def emit_L1_ffn(env, io):
    nc = env.nc
    wgT_d = nc.dram_tensor("l1f_wgT", [D, 2816], BF16, kind="Internal").ap()
    wuT_d = nc.dram_tensor("l1f_wuT", [D, 2816], BF16, kind="Internal").ap()
    wdT_d = nc.dram_tensor("l1f_wdT", [2816, D], BF16, kind="Internal").ap()
    xq_d = nc.dram_tensor("l1f_xq", [TO, D], BF16, kind="Internal").ap()
    emit_l1f_wquant(env, io, wgT_d, wuT_d, wdT_d)
    emit_l1f_xquant(env, io, xq_d)
    with contextlib.ExitStack() as st:
        xq_fm = st.enter_context(_sbuf(nc, "xq_fm", [128, DC * TO], BF16))
        h_fm = st.enter_context(_sbuf(nc, "h_fm", [128, 22 * TO], BF16))
        emit_build_xt(env, xq_fm, xq_d, None, full=False)
        emit_l1f_gateup(env, wgT_d, wuT_d, xq_fm, h_fm)
        emit_l1f_hquant(env, h_fm)
        emit_tok_proj_resid_ln(env, h_fm, 2816, wdT_d, io["x_own"],
                               io["x_out"])


def emit_moe_router(env, xto, wr_dram, ne, renorm, w_all, wsg_dram=None,
                    sg_all=None):
    nc = env.nc
    with contextlib.ExitStack() as st:
        wrs = st.enter_context(_sbuf(nc, "rt_wr", [128, DC * ne], BF16))
        wsgs = st.enter_context(_sbuf(nc, "rt_wsg", [128, DC], BF16))
        ee = st.enter_context(_sbuf(nc, "rt_e", [128, ne], F32))
        m1 = st.enter_context(_sbuf(nc, "rt_m1", [128, 1], F32))
        ssum = st.enter_context(_sbuf(nc, "rt_s", [128, 1], F32))
        mx8 = st.enter_context(_sbuf(nc, "rt_mx", [128, 8], F32))
        msk = st.enter_context(_sbuf(nc, "rt_mk", [128, ne], F32))
        pr = st.enter_context(_psum(nc, "rt_p", [128, ne], F32))
        pg = st.enter_context(_psum(nc, "rt_pg", [128, 1], F32))
        env.begin()
        d1 = env.dma("gpsimd", wrs[:, :].rearrange("p (kc e) -> p kc e", e=ne),
                     wr_dram.rearrange("(kc p) e -> p kc e", p=128))
        d2 = d1
        if wsg_dram is not None:
            d2 = env.dma("gpsimd", wsgs[:, :], wsg_dram)
        prev_exp = 0
        for i in range(NT):
            def mm(t, i=i):
                for kc in range(DC):
                    inst = t.matmul(pr[:, :],
                                    xto[:, kc * TO + i * 128: kc * TO + (i + 1) * 128],
                                    wrs[:, kc * ne:(kc + 1) * ne],
                                    start=(kc == 0),
                                    stop=(kc == DC - 1))
                return inst
            pv = env.op("tensor", mm, waits=[("dma_g", d1),
                                             ("act", env.cur("act")),
                                             ("vec", env.cur("vec"))],
                        inc="pe")
            c1 = env.op("vector", lambda v: v.tensor_reduce(
                m1[:, :], pr[:, :], AX.X, ALU.max),
                waits=[("pe", pv), ("act", prev_exp)], inc="vec")
            c2 = env.op("vector", lambda v: v.tensor_scalar(
                m1[:, :], m1[:, :], -1.0, 0.0, ALU.mult, ALU.add),
                waits=[("vec", c1)], inc="vec")
            c3 = env.op("scalar", lambda s: s.activation(
                ee[:, :], pr[:, :], AF.Exp, bias=m1[:, :],
                accum_out=ssum[:, :]),
                waits=[("vec", c2)], inc="act")
            prev_exp = c3
            c4 = env.op("vector", lambda v: v.reciprocal(ssum[:, :], ssum[:, :]),
                        waits=[("act", c3)], inc="vec")
            c5 = env.op("vector", lambda v: v.tensor_scalar(
                ee[:, :], ee[:, :], ssum[:, 0:1], None, ALU.mult),
                waits=[("vec", c4)], inc="vec")
            c6 = env.op("vector", lambda v: v.max(mx8[:, :], ee[:, :]),
                        waits=[("vec", c5)], inc="vec")
            c7 = env.op("vector", lambda v: v.tensor_scalar(
                msk[:, :], ee[:, :], mx8[:, 1:2], None, ALU.is_ge),
                waits=[("vec", c6)], inc="vec")
            c8 = env.op("vector", lambda v, i=i: v.tensor_tensor(
                w_all[:, i * ne:(i + 1) * ne], ee[:, :], msk[:, :], ALU.mult),
                waits=[("vec", c7)], inc="vec")
            if renorm:
                c9 = env.op("vector", lambda v: v.tensor_tensor(
                    m1[:, :], mx8[:, 0:1], mx8[:, 1:2], ALU.add),
                    waits=[("vec", c8), ("act", c3)], inc="vec")
                c10 = env.op("vector", lambda v: v.reciprocal(m1[:, :], m1[:, :]),
                             waits=[("vec", c9)], inc="vec")
                env.op("vector", lambda v, i=i: v.tensor_scalar(
                    w_all[:, i * ne:(i + 1) * ne], w_all[:, i * ne:(i + 1) * ne],
                    m1[:, 0:1], None, ALU.mult),
                    waits=[("vec", c10)], inc="vec")
            if wsg_dram is not None:
                def mmg(t, i=i):
                    for kc in range(DC):
                        inst = t.matmul(pg[:, :],
                                        xto[:, kc * TO + i * 128:
                                            kc * TO + (i + 1) * 128],
                                        wsgs[:, kc:kc + 1], start=(kc == 0),
                                        stop=(kc == DC - 1))
                    return inst
                gv = env.op("tensor", mmg,
                            waits=[("dma_g", d2), ("act", env.cur("act"))],
                            inc="pe")
                env.op("scalar", lambda s, i=i: s.activation(
                    sg_all[:, i:i + 1], pg[:, :], AF.Sigmoid),
                    waits=[("pe", gv)], inc="act")
        env.flush("moe_router")


def emit_moe_experts(env, xto, y_acc, w_all, ne, hid, wg_dram, wu_dram,
                     wd_dram, tgroups, first_write):
    nc = env.nc
    HC = hid // 128
    TG = TO // tgroups
    with contextlib.ExitStack() as st:
        wgb = [st.enter_context(_sbuf(nc, f"me_wg{i}", [128, D], BF16))
               for i in range(2)]
        wub = [st.enter_context(_sbuf(nc, f"me_wu{i}", [128, D], BF16))
               for i in range(2)]
        wdb = st.enter_context(_sbuf(nc, "me_wd", [128, HC * 1024], BF16))
        hsb = st.enter_context(_sbuf(nc, "me_h", [128, HC * TG], BF16))
        sg = st.enter_context(_sbuf(nc, "me_sg", [128, 512], BF16))
        pg = [st.enter_context(_psum(nc, f"me_pg{i}", [128, 512], F32))
              for i in range(2)]
        pu = [st.enter_context(_psum(nc, f"me_pu{i}", [128, 512], F32))
              for i in range(2)]
        pd = [st.enter_context(_psum(nc, f"me_pd{i}", [128, 512], F32))
              for i in range(4)]
        env.begin()
        wfree = [0, 0]
        pgf = [0, 0]
        puf = [0, 0]
        pdf = [0] * 4
        for e in range(ne):
            for tg in range(tgroups):
                t0 = tg * TG
                dwd = env.dma("gpsimd",
                              wdb[:, :].rearrange("p (hc o) -> p hc o", o=D),
                              wd_dram[e].rearrange("(hc p) o -> p hc o", p=128),
                              waits=[("pe", env.cur("pe"))])
                for m in range(HC):
                    bj = m % 2
                    d1 = env.dma("gpsimd",
                                 wgb[bj][:, :].rearrange("p (kc o) -> p kc o",
                                                         o=128),
                                 wg_dram[e][:, m * 128:(m + 1) * 128].rearrange(
                                     "(kc p) o -> p kc o", p=128),
                                 waits=[("pe", wfree[bj])])
                    d2 = env.dma("gpsimd",
                                 wub[bj][:, :].rearrange("p (kc o) -> p kc o",
                                                         o=128),
                                 wu_dram[e][:, m * 128:(m + 1) * 128].rearrange(
                                     "(kc p) o -> p kc o", p=128))
                    for g in range(TG // 512):
                        pj = g % 2
                        def mmg(t, bj=bj, g=g, pj=pj, t0=t0):
                            for kc in range(DC):
                                inst = t.matmul(
                                    pg[pj][:, :],
                                    wgb[bj][:, kc * 128:(kc + 1) * 128],
                                    xto[:, kc * TO + t0 + g * 512:
                                        kc * TO + t0 + g * 512 + 512],
                                    start=(kc == 0), stop=(kc == DC - 1))
                            return inst
                        g1 = env.op("tensor", mmg,
                                    waits=[("dma_g", d1), ("vec", pgf[pj])],
                                    inc="pe")
                        def mmu(t, bj=bj, g=g, pj=pj, t0=t0):
                            for kc in range(DC):
                                inst = t.matmul(
                                    pu[pj][:, :],
                                    wub[bj][:, kc * 128:(kc + 1) * 128],
                                    xto[:, kc * TO + t0 + g * 512:
                                        kc * TO + t0 + g * 512 + 512],
                                    start=(kc == 0), stop=(kc == DC - 1))
                            return inst
                        g2 = env.op("tensor", mmu,
                                    waits=[("dma_g", d2), ("vec", puf[pj])],
                                    inc="pe")
                        a1 = env.op("scalar",
                                    lambda s, pj=pj:
                                    s.activation(sg[:, :], pg[pj][:, :], AF.Silu),
                                    waits=[("pe", g1),
                                           ("vec", env.cur("vec"))],
                                    inc="act")
                        vv = env.op("vector",
                                    lambda v, pj=pj, m=m, g=g:
                                    v.tensor_tensor(
                                        hsb[:, m * TG + g * 512:
                                            m * TG + g * 512 + 512],
                                        sg[:, :], pu[pj][:, :], ALU.mult),
                                    waits=[("act", a1), ("pe", g2),
                                           ("pe", env.cur("pe"))], inc="vec")
                        pgf[pj] = vv
                        puf[pj] = vv
                    wfree[bj] = env.cur("pe")
                hv = env.cur("vec")
                for ti in range(TG // 128):
                    gi = (t0 + ti * 128) // 128
                    w_ap = w_all[:, gi * ne + e: gi * ne + e + 1]
                    for og in range(2):
                        pj = (ti * 2 + og) % 4
                        def mmd(t, ti=ti, og=og, pj=pj):
                            for hc in range(HC):
                                inst = t.matmul(
                                    pd[pj][:, :],
                                    hsb[:, hc * TG + ti * 128:
                                        hc * TG + (ti + 1) * 128],
                                    wdb[:, hc * D + og * 512:
                                        hc * D + og * 512 + 512],
                                    start=(hc == 0), stop=(hc == HC - 1))
                            return inst
                        dv = env.op("tensor", mmd,
                                    waits=[("dma_g", dwd), ("vec", hv),
                                           ("vec", pdf[pj])], inc="pe")
                        dst = y_acc[:, gi * 1024 + og * 512:
                                    gi * 1024 + og * 512 + 512]
                        if first_write and e == 0:
                            ev = env.op("vector",
                                        lambda v, pj=pj, dst=dst, w_ap=w_ap:
                                        v.tensor_scalar(dst, pd[pj][:, :],
                                                        w_ap, None, ALU.mult),
                                        waits=[("pe", dv)], inc="vec")
                        else:
                            ev = env.op("vector",
                                        lambda v, pj=pj, dst=dst, w_ap=w_ap:
                                        v.scalar_tensor_tensor(
                                            dst, pd[pj][:, :], w_ap, dst,
                                            ALU.mult, ALU.add),
                                        waits=[("pe", dv)], inc="vec")
                        pdf[pj] = ev
        env.flush("moe_experts")


def emit_shared_expert(env, xto, y_acc, sg_all, wg_dram, wu_dram, wd_dram,
                       hid=2048):
    nc = env.nc
    HC = hid // 128
    with contextlib.ExitStack() as st:
        wgb = [st.enter_context(_sbuf(nc, f"sh_wg{i}", [128, D], BF16))
               for i in range(2)]
        wub = [st.enter_context(_sbuf(nc, f"sh_wu{i}", [128, D], BF16))
               for i in range(2)]
        wdb = st.enter_context(_sbuf(nc, "sh_wd", [128, HC * D], BF16))
        hsb = st.enter_context(_sbuf(nc, "sh_h", [128, HC * 512], BF16))
        sg = st.enter_context(_sbuf(nc, "sh_sg", [128, 512], BF16))
        pg = [st.enter_context(_psum(nc, f"sh_pg{i}", [128, 512], F32))
              for i in range(2)]
        pu = [st.enter_context(_psum(nc, f"sh_pu{i}", [128, 512], F32))
              for i in range(2)]
        pd = [st.enter_context(_psum(nc, f"sh_pd{i}", [128, 512], F32))
              for i in range(4)]
        env.begin()
        wd_ld = env.dma("gpsimd",
                        wdb[:, :].rearrange("p (hc o) -> p hc o", o=D),
                        wd_dram.rearrange("(hc p) o -> p hc o", p=128))
        wfree = [0, 0]
        pgf = [0, 0]
        puf = [0, 0]
        pdf = [0] * 4
        for tg in range(4):
            t0 = tg * 512
            for m in range(HC):
                bj = m % 2
                d1 = env.dma("gpsimd",
                             wgb[bj][:, :].rearrange("p (kc o) -> p kc o", o=128),
                             wg_dram[:, m * 128:(m + 1) * 128].rearrange(
                                 "(kc p) o -> p kc o", p=128),
                             waits=[("pe", wfree[bj])])
                d2 = env.dma("gpsimd",
                             wub[bj][:, :].rearrange("p (kc o) -> p kc o", o=128),
                             wu_dram[:, m * 128:(m + 1) * 128].rearrange(
                                 "(kc p) o -> p kc o", p=128))
                pj = m % 2
                def mmg(t, bj=bj, pj=pj, t0=t0):
                    for kc in range(DC):
                        inst = t.matmul(
                            pg[pj][:, :], wgb[bj][:, kc * 128:(kc + 1) * 128],
                            xto[:, kc * TO + t0: kc * TO + t0 + 512],
                            start=(kc == 0), stop=(kc == DC - 1))
                    return inst
                g1 = env.op("tensor", mmg,
                            waits=[("dma_g", d1), ("vec", pgf[pj])], inc="pe")
                def mmu(t, bj=bj, pj=pj, t0=t0):
                    for kc in range(DC):
                        inst = t.matmul(
                            pu[pj][:, :], wub[bj][:, kc * 128:(kc + 1) * 128],
                            xto[:, kc * TO + t0: kc * TO + t0 + 512],
                            start=(kc == 0), stop=(kc == DC - 1))
                    return inst
                g2 = env.op("tensor", mmu,
                            waits=[("dma_g", d2), ("vec", puf[pj])], inc="pe")
                a1 = env.op("scalar",
                            lambda s, pj=pj:
                            s.activation(sg[:, :], pg[pj][:, :], AF.Silu),
                            waits=[("pe", g1), ("vec", env.cur("vec"))],
                            inc="act")
                vv = env.op("vector",
                            lambda v, pj=pj, m=m:
                            v.tensor_tensor(hsb[:, m * 512:(m + 1) * 512],
                                            sg[:, :], pu[pj][:, :], ALU.mult),
                            waits=[("act", a1), ("pe", g2)], inc="vec")
                pgf[pj] = vv
                puf[pj] = vv
                wfree[bj] = env.cur("pe")
            hv = env.cur("vec")
            for ti in range(4):
                gi = (t0 + ti * 128) // 128
                sg_ap = sg_all[:, gi:gi + 1]
                for og in range(2):
                    pj = (ti * 2 + og) % 4
                    def mmd(t, ti=ti, og=og, pj=pj):
                        for hc in range(HC):
                            inst = t.matmul(
                                pd[pj][:, :],
                                hsb[:, hc * 512 + ti * 128:
                                    hc * 512 + (ti + 1) * 128],
                                wdb[:, hc * D + og * 512:
                                    hc * D + og * 512 + 512],
                                start=(hc == 0), stop=(hc == HC - 1))
                        return inst
                    dv = env.op("tensor", mmd,
                                waits=[("vec", hv), ("vec", pdf[pj]),
                                       ("dma_g", wd_ld)], inc="pe")
                    dst = y_acc[:, gi * 1024 + og * 512:
                                gi * 1024 + og * 512 + 512]
                    ev = env.op("vector",
                                lambda v, pj=pj, dst=dst, sg_ap=sg_ap:
                                v.scalar_tensor_tensor(
                                    dst, pd[pj][:, :], sg_ap, dst,
                                    ALU.mult, ALU.add),
                                waits=[("pe", dv)], inc="vec")
                    pdf[pj] = ev
        env.flush("shared_expert")


def emit_resid_ln_from_tok(env, y_tok, x_own, x_out, eps=1e-5):
    nc = env.nc
    with contextlib.ExitStack() as st:
        xbuf = [st.enter_context(_sbuf(nc, f"xr{i}", [128, D], F32))
                for i in range(2)]
        xn = [st.enter_context(_sbuf(nc, f"xn{i}", [128, D], F32))
              for i in range(2)]
        stats = st.enter_context(_sbuf(nc, "stats", [128, 2 * NT], F32))
        sumsq = st.enter_context(_sbuf(nc, "sumsq", [128, 2 * NT], F32))
        mean = st.enter_context(_sbuf(nc, "mean", [128, NT], F32))
        rstd = st.enter_context(_sbuf(nc, "rstd", [128, NT], F32))
        nmr = st.enter_context(_sbuf(nc, "nmr", [128, NT], F32))
        sq = st.enter_context(_sbuf(nc, "sqs", [128, 512], BF16))
        env.begin()
        xfree = [0, 0]
        sqfree = 0
        for i in range(NT):
            bj = i % 2
            dx = env.dma("sync", xbuf[bj][:, :],
                         x_own[i * 128:(i + 1) * 128, :],
                         waits=[("out", xfree[bj])])
            for g in range(2):
                vv = env.op("vector",
                            lambda v, bj=bj, g=g, i=i:
                            v.scalar_tensor_tensor(
                                xn[bj][:, g * 512:(g + 1) * 512],
                                y_tok[:, i * 1024 + g * 512:
                                      i * 1024 + g * 512 + 512], 1.0,
                                xbuf[bj][:, g * 512:(g + 1) * 512],
                                ALU.mult, ALU.add,
                                accum_out=stats[:, 2 * i + g:2 * i + g + 1]),
                            waits=[("dma_s", dx), ("out", xfree[bj])],
                            inc="vec")
                sv = env.op("scalar",
                            lambda s, bj=bj, i=i, g=g:
                            s.activation(sq[:, :],
                                         xn[bj][:, g * 512:(g + 1) * 512],
                                         AF.Square,
                                         accum_out=sumsq[:, 2 * i + g:
                                                         2 * i + g + 1]),
                            waits=[("vec", vv), ("act", sqfree)], inc="act")
                sqfree = sv
            ap1 = _ln_tail(env, i, bj, xn, stats, sumsq, mean, rstd, nmr, eps)
            xfree[bj] = env.op("sync",
                               lambda e, bj=bj, i=i:
                               e.dma_start(out=x_out[i * 128:(i + 1) * 128, :],
                                           in_=xn[bj][:, :]),
                               waits=[("act", ap1)], inc="out", k=16)
        env.wait_all("sync", "out")
        env.flush("resid_ln_tok")


def emit_L0_ffn(env, io):
    nc = env.nc
    with contextlib.ExitStack() as st:
        xto = st.enter_context(_sbuf(nc, "xto", [128, DC * TO], BF16))
        y_acc = st.enter_context(_sbuf(nc, "l0f_y", [128, NT * 1024], F32))
        w_all = st.enter_context(_sbuf(nc, "l0f_w", [128, NT * 16], F32))
        sg_all = st.enter_context(_sbuf(nc, "l0f_sga", [128, NT], F32))
        emit_build_xt(env, xto, io["x_own"], None, full=False)
        emit_moe_router(env, xto, io["L0f_WgT"], 16, True, w_all,
                        wsg_dram=io["L0f_WsgT"], sg_all=sg_all)
        emit_moe_experts(env, xto, y_acc, w_all, 16, 512, io["L0f_WegT"],
                         io["L0f_WeuT"], io["L0f_WedT"], 1, True)
        emit_shared_expert(env, xto, y_acc, sg_all, io["L0f_WsgateT"],
                           io["L0f_WsupT"], io["L0f_WsdownT"])
        emit_resid_ln_from_tok(env, y_acc, io["x_own"], io["x_out"])


def emit_L3_ffn(env, io):
    nc = env.nc
    with contextlib.ExitStack() as st:
        xto = st.enter_context(_sbuf(nc, "xto", [128, DC * TO], BF16))
        y_acc = st.enter_context(_sbuf(nc, "l3f_y", [128, NT * 1024], F32))
        w_all = st.enter_context(_sbuf(nc, "l3f_w", [128, NT * 8], F32))
        emit_build_xt(env, xto, io["x_own"], None, full=False)
        emit_moe_router(env, xto, io["L3f_wrT"], 8, False, w_all)
        emit_moe_experts(env, xto, y_acc, w_all, 8, 2816, io["L3f_WgT"],
                         io["L3f_WuT"], io["L3f_WdT"], 2, True)
        emit_resid_ln_from_tok(env, y_acc, io["x_own"], io["x_out"])



BLOCK_DEFS = {
    "L1a": (
        {"L1_attn_WT": [D, 3 * D], "L1_attn_bc": [128, 24]},
        emit_L1_attn, []),
    "L2a": (
        {"L2_attn_qT": [D, D], "L2_attn_kT": [D, D], "L2_attn_vT": [D, D],
         "L2_attn_oT": [D, D]},
        emit_L2_attn, []),
    "L0a": (
        {"L0_attn_WqT": [D, D], "L0_attn_Wq2T": [D, D], "L0_attn_WkT": [D, D],
         "L0_attn_Wk2T": [D, D], "L0_attn_WvT": [D, D], "L0_attn_WoT": [D, D],
         "L0_attn_mixb": [128, 1], "L0_attn_qsb": [128, 1]},
        emit_L0_attn, []),
    "L0f": (
        {"L0f_WgT": [D, 16], "L0f_WsgT": [128, 8],
         "L0f_WegT": [16, D, 512], "L0f_WeuT": [16, D, 512],
         "L0f_WedT": [16, 512, D],
         "L0f_WsgateT": [D, 2048], "L0f_WsupT": [D, 2048],
         "L0f_WsdownT": [2048, D]},
        emit_L0_ffn, []),
    "L3f": (
        {"L3f_wrT": [D, 8], "L3f_WgT": [8, D, 2816], "L3f_WuT": [8, D, 2816],
         "L3f_WdT": [8, 2816, D]},
        emit_L3_ffn, []),
    "L1f": (
        {"L1_ffn_w_gate": [2816, D], "L1_ffn_w_up": [2816, D],
         "L1_ffn_w_down": [D, 2816]},
        emit_L1_ffn, []),
    "L2f": (
        {"L2f_srT": [D, 64], "L2f_knotb": [128, 8], "L2f_tempb": [128, 1],
         "L2f_cw1": [D, 256], "L2f_cb1c": [128, 2], "L2f_cw2": [256, 2],
         "L2f_cb2c": [2, 1], "L2f_w1all": [2, 1024], "L2f_b1bc": [128, 1024],
         "L2f_w2bc": [128, 1024], "L2f_b2bc": [128, 64], "L2f_dirs": [64, 1024],
         "L2f_gbc": [128, D], "L2f_bbc": [128, D], "L2f_osb": [128, 1]},
        emit_L2_ffn, ["c_iota64"]),
    "L3a": (
        {"L3_attn_wq": [D, 256], "L3_attn_wqc": [256, 896],
         "L3_attn_wqr": [D, 128], "L3_attn_wkv": [D, 256],
         "L3_attn_wkr": [D, 128], "L3_attn_wkc": [256, 896],
         "L3_attn_wv": [256, D], "L3_attn_wo": [D, D],
         "L3_attn_bias": [128, 28], "L3_attn_bo_bc": [128, D]},
        emit_L3_attn, ["c_ropecos", "c_ropesin", "c_ropecos_o", "c_ropesin_o"]),
}


def build_launch(blocks):
    nc = bass.Bass("TRN2", target_bir_lowering=False)
    io = {}
    consts = host_consts()
    cap = {}
    needed = set(BASE_CONSTS)
    for b in blocks:
        needed |= set(BLOCK_DEFS[b][2])
    for nm in sorted(needed):
        shape, _, _ = CONST_SPECS[nm]
        cap[nm] = nc.dram_tensor(nm, shape, F32, kind="ExternalInput").ap()
    for key in ("x_own", "x_ev", "x_od"):
        io[key] = nc.dram_tensor(key, [TO, D], F32, kind="ExternalInput").ap()
    for b in blocks:
        for nm, shape in BLOCK_DEFS[b][0].items():
            io[nm] = nc.dram_tensor(nm, shape, F32, kind="ExternalInput").ap()
    env = Env(nc)
    emit_consts(env, cap, sorted(needed))
    for j, b in enumerate(blocks):
        if j + 1 == len(blocks):
            io["x_out"] = nc.dram_tensor("x_out", [TO, D], F32,
                                         kind="ExternalOutput").ap()
        else:
            io["x_out"] = nc.dram_tensor(f"x_mid{j}", [TO, D], F32,
                                         kind="ExternalOutput" if DBG[0]
                                         else "Internal").ap()
        BLOCK_DEFS[b][1](env, io)
        io["x_own"] = io["x_out"]
    env.ctx.close()
    return nc


def prep_weights(inputs):
    w = {}
    if "L1_attn_W" in inputs:
        w["L1_attn_WT"] = _t(inputs["L1_attn_W"])
        w["L1_attn_bc"] = _bias_cols(inputs["L1_attn_b"])
    if "L0_attn_Wq" in inputs:
        for nm in ("Wq", "Wk", "Wv", "Wq2", "Wk2", "Wo"):
            w["L0_attn_" + nm + "T"] = _t(inputs["L0_attn_" + nm])
        w["L0_attn_mixb"] = np.ascontiguousarray(np.broadcast_to(
            np.asarray(inputs["L0_attn_mixture"], dtype=np.float32)
            .reshape(1, 1), (128, 1)))
        w["L0_attn_qsb"] = np.ascontiguousarray(np.broadcast_to(
            np.asarray(inputs["L0_attn_quartet_scale"], dtype=np.float32)
            .reshape(1, 1), (128, 1)))
    if "L0_ffn_Wg" in inputs:
        w["L0f_WgT"] = _t(inputs["L0_ffn_Wg"])
        w["L0f_WsgT"] = np.ascontiguousarray(np.asarray(
            inputs["L0_ffn_Wsg"], dtype=np.float32).reshape(8, 128).T)
        w["L0f_WegT"] = np.ascontiguousarray(np.asarray(
            inputs["L0_ffn_We_gate"], dtype=np.float32).transpose(0, 2, 1))
        w["L0f_WeuT"] = np.ascontiguousarray(np.asarray(
            inputs["L0_ffn_We_up"], dtype=np.float32).transpose(0, 2, 1))
        w["L0f_WedT"] = np.ascontiguousarray(np.asarray(
            inputs["L0_ffn_We_down"], dtype=np.float32).transpose(0, 2, 1))
        w["L0f_WsgateT"] = _t(inputs["L0_ffn_Ws_gate"])
        w["L0f_WsupT"] = _t(inputs["L0_ffn_Ws_up"])
        w["L0f_WsdownT"] = _t(inputs["L0_ffn_Ws_down"])
    if "L3_ffn_router_w" in inputs:
        w["L3f_wrT"] = _t(inputs["L3_ffn_router_w"])
        w["L3f_WgT"] = np.ascontiguousarray(np.asarray(
            inputs["L3_ffn_gate_w"], dtype=np.float32).transpose(0, 2, 1))
        w["L3f_WuT"] = np.ascontiguousarray(np.asarray(
            inputs["L3_ffn_up_w"], dtype=np.float32).transpose(0, 2, 1))
        w["L3f_WdT"] = np.ascontiguousarray(np.asarray(
            inputs["L3_ffn_down_w"], dtype=np.float32).transpose(0, 2, 1))
    if "L1_ffn_w_gate" in inputs:
        for nm in ("w_gate", "w_up", "w_down"):
            w["L1_ffn_" + nm] = np.ascontiguousarray(
                np.asarray(inputs["L1_ffn_" + nm], dtype=np.float32))
    if "L2_ffn_signatures_raw" in inputs:
        def bc(v, shape):
            return np.ascontiguousarray(np.broadcast_to(
                np.asarray(v, dtype=np.float32).reshape(1, -1), shape)
                if np.asarray(v).size > 1 or shape[1] == 1 and False
                else np.broadcast_to(
                    np.asarray(v, dtype=np.float32).reshape(1, -1), shape))
        w["L2f_srT"] = _t(inputs["L2_ffn_signatures_raw"])
        w["L2f_knotb"] = bc(inputs["L2_ffn_knot_values"], (128, 8))
        w["L2f_tempb"] = bc(inputs["L2_ffn_temperature"], (128, 1))
        w["L2f_cw1"] = np.ascontiguousarray(
            np.asarray(inputs["L2_ffn_compress_w1"], dtype=np.float32))
        w["L2f_cb1c"] = _bias_cols(inputs["L2_ffn_compress_b1"])
        w["L2f_cw2"] = np.ascontiguousarray(
            np.asarray(inputs["L2_ffn_compress_w2"], dtype=np.float32))
        w["L2f_cb2c"] = np.ascontiguousarray(
            np.asarray(inputs["L2_ffn_compress_b2"], dtype=np.float32)
            .reshape(2, 1))
        w["L2f_w1all"] = np.ascontiguousarray(
            np.asarray(inputs["L2_ffn_spline_w1"], dtype=np.float32)
            .transpose(1, 0, 2).reshape(2, 1024))
        w["L2f_b1bc"] = bc(np.asarray(inputs["L2_ffn_spline_b1"]).reshape(-1),
                           (128, 1024))
        w["L2f_w2bc"] = bc(np.asarray(inputs["L2_ffn_spline_w2"]).reshape(-1),
                           (128, 1024))
        w["L2f_b2bc"] = bc(np.asarray(inputs["L2_ffn_spline_b2"]).reshape(-1),
                           (128, 64))
        w["L2f_dirs"] = np.ascontiguousarray(
            np.asarray(inputs["L2_ffn_directions"], dtype=np.float32))
        w["L2f_gbc"] = bc(inputs["L2_ffn_norm_gamma"], (128, D))
        w["L2f_bbc"] = bc(inputs["L2_ffn_norm_beta"], (128, D))
        w["L2f_osb"] = bc(inputs["L2_ffn_output_scale"], (128, 1))
    if "L3_attn_wq" in inputs:
        for nm in ("wq", "wqc", "wqr", "wkv", "wkr", "wkc", "wv", "wo"):
            w["L3_attn_" + nm] = np.ascontiguousarray(
                np.asarray(inputs["L3_attn_" + nm], dtype=np.float32))
        cols = []
        for nm in ("bq", "bqc", "bqr", "bkv", "bkr", "bkc", "bv"):
            cols.append(_bias_cols(inputs["L3_attn_" + nm]))
        w["L3_attn_bias"] = np.ascontiguousarray(np.concatenate(cols, axis=1))
        w["L3_attn_bo_bc"] = np.ascontiguousarray(np.broadcast_to(
            np.asarray(inputs["L3_attn_bo"], dtype=np.float32)[None, :],
            (128, D)))
    if "L2_attn_q_proj" in inputs:
        w["L2_attn_qT"] = _t(inputs["L2_attn_q_proj"])
        w["L2_attn_kT"] = _t(inputs["L2_attn_k_proj"])
        w["L2_attn_vT"] = _t(inputs["L2_attn_v_proj"])
        w["L2_attn_oT"] = _t(inputs["L2_attn_o_proj"])
    return w


_BLOCK_SEQ = [["L0a", "L0f"], ["L1a", "L1f"], ["L2a", "L2f"], ["L3a"], ["L3f"]]


def run_chain(x_full, inputs, block_seq=None, n_launches_out=None):
    block_seq = block_seq or _BLOCK_SEQ
    consts = host_consts()
    w = prep_weights(inputs)
    rows = [own_row_index(c) for c in range(NCORES)]
    xc = [np.ascontiguousarray(x_full[c // 2].reshape(T, D)[rows[c]])
          for c in range(NCORES)]
    for blocks in block_seq:
        nc = build_launch(blocks)
        in_maps = []
        for c in range(NCORES):
            m = dict(consts)
            m["c_diag2"] = host_diag2(c % 2)
            m["c_ropecos_o"], m["c_ropesin_o"] = host_rope_own(c % 2)
            m["x_own"] = xc[c]
            m["x_ev"] = xc[2 * (c // 2)]
            m["x_od"] = xc[2 * (c // 2) + 1]
            names = set()
            for b in blocks:
                names |= set(BLOCK_DEFS[b][0])
            for nm in names:
                m[nm] = w[nm]
            in_maps.append(m)
        res = run_bass_kernel_spmd(nc, in_maps, core_ids=list(range(NCORES)))
        xc = [np.ascontiguousarray(np.asarray(res.results[c]["x_out"],
                                              dtype=np.float32))
              for c in range(NCORES)]
    out = np.empty((B, T, D), np.float32)
    for c in range(NCORES):
        out[c // 2].reshape(T, D)[rows[c]] = xc[c]
    return out


REPLICA_GROUPS = [[0, 1], [2, 3], [4, 5], [6, 7]]


def emit_exchange(env, io, tag):
    nc = env.nc
    cc_out = nc.dram_tensor(f"cc_out_{tag}", [2 * TO, D], F32,
                            kind="Internal").ap()
    cc_buf = nc.dram_tensor(f"cc_buf_{tag}", [2 * TO, D], F32,
                            kind="Internal").ap()
    x_own = io["x_own"]
    env.begin()
    v = env.op("gpsimd",
               lambda g: g.collective_compute(
                   "AllGather", ALU.bypass, ins=[x_own], outs=[cc_out],
                   replica_groups=REPLICA_GROUPS),
               waits=[("out", env.cur("out")), ("dma_s", env.cur("dma_s"))],
               inc="cc")
    for e in ENGINES:
        env.op(e, lambda g: None, waits=[("cc", v)])
    env.flush("exchange")
    env.begin()
    env.dma("gpsimd", cc_buf, cc_out, waits=[("cc", v)])
    env.wait_all("gpsimd", "dma_g")
    env.flush("exchange_bounce")
    io["x_ev"] = cc_buf[0:TO, :]
    io["x_od"] = cc_buf[TO:2 * TO, :]


FUSED_SEQ = ["L0a", "L0f", "L1a", "L1f", "L2a", "L2f", "L3a", "L3f"]


def build_fused(seq=None):
    FS = seq or FUSED_SEQ
    nc = bass.Bass("TRN2", target_bir_lowering=False)
    io = {}
    cap = {}
    all_needed = set(BASE_CONSTS)
    for b in FS:
        all_needed |= set(BLOCK_DEFS[b][2])
    for nm in sorted(all_needed):
        shape, _, _ = CONST_SPECS[nm]
        cap[nm] = nc.dram_tensor(nm, shape, F32, kind="ExternalInput").ap()
    for key in ("x_own", "x_ev", "x_od"):
        io[key] = nc.dram_tensor(key, [TO, D], F32, kind="ExternalInput").ap()
    for b in FS:
        for nm, shape in BLOCK_DEFS[b][0].items():
            io[nm] = nc.dram_tensor(nm, shape, F32, kind="ExternalInput").ap()
    env = Env(nc)
    emit_consts(env, cap, sorted(BASE_CONSTS))
    for j, b in enumerate(FS):
        if j + 1 == len(FS):
            io["x_out"] = nc.dram_tensor("x_out", [TO, D], F32,
                                         kind="ExternalOutput").ap()
        else:
            io["x_out"] = nc.dram_tensor(f"x_mid{j}", [TO, D], F32,
                                         kind="Internal").ap()
        if b in ("L1a", "L2a", "L3a"):
            emit_exchange(env, io, b)
        with contextlib.ExitStack() as bs:
            env.res_stack = bs
            extra = [nm for nm in BLOCK_DEFS[b][2] if nm not in env.consts]
            if extra:
                emit_consts(env, cap, extra)
            BLOCK_DEFS[b][1](env, io)
            env.res_stack = None
            for nm in BLOCK_DEFS[b][2]:
                env.consts.pop(nm, None)
        io["x_own"] = io["x_out"]
    env.ctx.close()
    return nc


_FUSED_NC = [None]


def run_fused(x_full, inputs, seq=None):
    consts = host_consts()
    w = prep_weights(inputs)
    rows = [own_row_index(c) for c in range(NCORES)]
    xc = [np.ascontiguousarray(x_full[c // 2].reshape(T, D)[rows[c]])
          for c in range(NCORES)]
    if seq is not None:
        nc = build_fused(seq)
    else:
        if _FUSED_NC[0] is None:
            _FUSED_NC[0] = build_fused()
        nc = _FUSED_NC[0]
    in_maps = []
    for c in range(NCORES):
        m = dict(consts)
        m["c_diag2"] = host_diag2(c % 2)
        m["c_ropecos_o"], m["c_ropesin_o"] = host_rope_own(c % 2)
        m["x_own"] = xc[c]
        m["x_ev"] = xc[2 * (c // 2)]
        m["x_od"] = xc[2 * (c // 2) + 1]
        m.update(w)
        in_maps.append(m)
    res = run_bass_kernel_spmd(nc, in_maps, core_ids=list(range(NCORES)))
    out = np.empty((B, T, D), np.float32)
    for c in range(NCORES):
        out[c // 2].reshape(T, D)[rows[c]] = np.ascontiguousarray(
            np.asarray(res.results[c]["x_out"], dtype=np.float32))
    return out


FUSED = True


def kernel(**inputs):
    x = np.ascontiguousarray(np.asarray(inputs["L0_attn_x"], dtype=np.float32))
    if FUSED:
        return run_fused(x, inputs)
    return run_chain(x, inputs)


if __name__ == "__main__":
    pass
